# version 08
# baseline (speedup 1.0000x reference)
import numpy as np
from contextlib import ExitStack

import concourse.bass as bass
import concourse.mybir as mybir
from concourse.bass_utils import run_bass_kernel_spmd

F32 = mybir.dt.float32
BF16 = mybir.dt.bfloat16
AF = mybir.ActivationFunctionType
ALU = mybir.AluOpType
AX = mybir.AxisListType

B, S, D, NH, DH = 4, 4096, 1024, 16, 64
FF = 3072
HPC, TOK, NT, DC, FC = 8, 2048, 32, 8, 24
QCH = 8
MAGIC = 12582912.0
QA, QW = 32767.0, 7.0
NEG = -1.0e9
U16 = 65536.0


def build_nc(dbg=False):
    nc = bass.Bass("TRN2", target_bir_lowering=False, debug=False,
                   detect_race_conditions=False)

    x_h = nc.dram_tensor("x", [S, D], F32, kind="ExternalInput").ap()
    xm_h = nc.dram_tensor("xmine", [TOK, D], F32, kind="ExternalInput").ap()
    wqkv_h = nc.dram_tensor("wqkv", [D, 1536], F32, kind="ExternalInput").ap()
    bqkv_h = nc.dram_tensor("bqkv", [1, 1536], F32, kind="ExternalInput").ap()
    wout_h = nc.dram_tensor("wout", [64, NH * D], F32, kind="ExternalInput").ap()
    bout_h = nc.dram_tensor("bout", [1, D], F32, kind="ExternalInput").ap()
    hsel_h = nc.dram_tensor("hsel", [1, 2], F32, kind="ExternalInput").ap()
    wg_h = nc.dram_tensor("wg", [FF, D], F32, kind="ExternalInput").ap()
    wu_h = nc.dram_tensor("wu", [FF, D], F32, kind="ExternalInput").ap()
    wd_h = nc.dram_tensor("wd", [D, FF], F32, kind="ExternalInput").ap()
    out_h = nc.dram_tensor("out", [TOK, D], F32, kind="ExternalOutput").ap()

    xn16_h = nc.dram_tensor("xn16", [S, D], BF16).ap()
    xq16_h = nc.dram_tensor("xq16", [TOK, D], BF16).ap()
    qT_h = nc.dram_tensor("qTs", [4 * 128, S], BF16).ap()
    w16_h = {"u": nc.dram_tensor("wu16", [FF, D], BF16).ap(),
             "g": nc.dram_tensor("wg16", [FF, D], BF16).ap(),
             "d": nc.dram_tensor("wd16", [D, FF], BF16).ap()}
    fup_h = nc.dram_tensor("fup", [FF, TOK], F32).ap()
    fga_h = nc.dram_tensor("fgate", [FF, TOK], F32).ap()
    fg2_h = nc.dram_tensor("fg2", [FF, TOK], F32).ap()
    foq_h = nc.dram_tensor("foq", [FF, TOK], BF16).ap()
    ansr_h = nc.dram_tensor("ansraw", [HPC * DH, S], BF16).ap()
    lbuf_h = nc.dram_tensor("lbuf", [1, HPC * S], F32).ap()
    ansm_h = nc.dram_tensor("ansmine", [NH * DH, TOK], BF16).ap()
    cc_i = {k: nc.dram_tensor(f"cc{k}_i", [128, 2 if k == "B" else 1], F32)
            for k in "ABCD"}
    cc_o = {k: nc.dram_tensor(f"cc{k}_o", [128, 2 if k == "B" else 1], F32)
            for k in "ABCD"}
    exi_h = nc.dram_tensor("exch_i", [HPC * DH, S], BF16)
    exo_h = nc.dram_tensor("exch_o", [NH * DH, S], BF16)
    dbg_h = {}
    if dbg:
        for nm, shp, dty in (("dxq", [TOK, D], BF16), ("dwu", [FF, D], BF16),
                             ("dfga", [FF, TOK], F32), ("dfg2", [FF, TOK], F32),
                             ("dfoq", [FF, TOK], BF16), ("dfup", [FF, TOK], F32)):
            dbg_h[nm] = nc.dram_tensor(nm, shp, dty, kind="ExternalOutput").ap()

    with ExitStack() as ctx:
        E = ctx.enter_context

        s1t = E(nc.sbuf_tensor("s1t", [1, 1936], F32))
        onesf = s1t[0:1, 1808:1936]
        bq_sb = E(nc.sbuf_tensor("bq_sb", [128, 12], F32))
        scl = E(nc.sbuf_tensor("scl", [128, 16], F32))
        hselb = E(nc.sbuf_tensor("hselb", [128, 2], F32))
        stats = E(nc.sbuf_tensor("stats", [128, 16], F32))
        mxt = E(nc.sbuf_tensor("mxt", [128, 16], F32))
        onehot = E(nc.sbuf_tensor("onehot", [HPC, HPC * 64], F32))
        constv = E(nc.sbuf_tensor("constv", [128, 2], F32))

        A1 = E(nc.sbuf_tensor("A1", [128, 24576], BF16))
        A2 = E(nc.sbuf_tensor("A2", [128, 20480], BF16))
        A3 = E(nc.sbuf_tensor("A3", [128, 42240], BF16))
        A4 = E(nc.sbuf_tensor("A4", [128, 12288], BF16))
        PS = E(nc.psum_tensor("PS", [128, 4096], F32))

        def bank(i, n=1):
            return PS[:, 512 * i: 512 * (i + n)]

        wqb = A1[:, 0:12288]
        xnT = A1[:, 12288:20480]
        wq32 = A1[:, 20480:23552].bitcast(F32)
        WT = A1[:, 0:24576]
        xring = A2[:, 0:4096].bitcast(F32)
        xnbuf = A2[:, 4096:8192].bitcast(F32)
        xn16b = A2[:, 8192:10240]
        xq16b = A2[:, 10240:12288]
        qstg = A2[:, 12288:13312]
        vbias = A2[:, 13312:14352].bitcast(F32)
        sblk = A4[:, 10240:12288].bitcast(F32)
        xqT = A2[:, 0:16384]
        fst = A2[:, 16384:20480].bitcast(F32)
        ein = A2[:, 0:8192].bitcast(F32)
        ein2 = A2[:, 8192:16384].bitcast(F32)
        eo16 = A2[:, 16384:20480]
        KT = A3[:, 0:16384]
        VT = A3[:, 16384:33024]
        PTa = A3[:, 33024:37120]
        qsl = A3[:, 37120:38144]
        ansst = A3[0:65, 38144:39168]
        mask2 = A3[:, 39168:40192].bitcast(F32)
        lstg = A3[0:65, 40192:42240].bitcast(F32)
        l8 = A3[0:8, 0:8192].bitcast(F32)
        blh = A3[0:64, 8192:12288]
        blst = A3[0:64, 12288:14336]
        anr = A3[0:64, 14336:18432]
        ansl = A3[0:64, 38144:42240]
        wosb = A3[0:64, 0:32768]
        dq1 = A4[:, 0:4096].bitcast(F32)
        dq2 = A4[:, 4096:8192].bitcast(F32)
        dq3 = A4[:, 8192:10240]
        t1b = A4[:, 0:4096].bitcast(F32)
        wost = A4[0:64, 0:2048].bitcast(F32)
        xfin = A4[:, 4096:6144].bitcast(F32)
        boutbc = A3[:, 32768:36864].bitcast(F32)
        oqsl = A4[:, 6144:12288]

        sem_names = [
            "d_x", "d_wq", "d_b", "d_wn", "d_xn16", "d_xnT", "d_xm", "d_xq",
            "d_xqT", "d_wdq", "d_WTu", "d_WTg", "d_WTd", "d_fup", "d_fga",
            "d_p2", "d_fg2", "d_p3", "d_p4", "d_foq", "d_oq", "d_wo", "d_xf",
            "d_out", "d_ans", "d_l", "d_l8", "d_ci", "d_co", "d_exi", "d_abl",
            "d_qT", "d_q", "d_am",
            "c_A", "c_B", "c_C", "c_D", "c_AG",
            "e_stp", "e_sq", "e_var", "e_sd", "e_xn", "e_cast", "e_qkv",
            "e_qev", "e_vev", "e_sc", "e_exp", "e_av", "e_avev", "e_deq",
            "e_scl", "e_xq", "e_fev", "e_fmx", "e_sig", "e_ew", "e_p3",
            "e_p4", "e_nrm", "e_bl", "e_dp", "e_op", "e_fin", "e_il",
            "e_msk", "e_vbc", "e_gmm", "e_sA", "e_sB", "e_sC", "e_sD",
            "e_upm", "e_woc", "e_bb", "e_p3g", "e_vch", "e_gch",
            "e_dqr", "e_dqg", "e_dqb",
        ]
        sems = {n: E(nc.semaphore(n)) for n in sem_names}
        cnt = {n: 0 for n in sem_names}
        prog = {k: [] for k in ("sync", "scalar", "vector", "tensor", "gpsimd")}

        def em(eng, fn):
            prog[eng].append(fn)

        def inc(name, k=1):
            cnt[name] += k
            return cnt[name]

        def wt(eng, name, val=None):
            v = cnt[name] if val is None else val
            if v <= 0:
                return
            em(eng, lambda e, s=sems[name], v=v: e.wait_ge(s, v))

        def dma(eng, out, in_, sem, transpose=False):
            s = sems[sem]
            if transpose:
                em(eng, lambda e, o=out, i=in_, s=s:
                   e.dma_start_transpose(o, i).then_inc(s, 16))
            else:
                em(eng, lambda e, o=out, i=in_, s=s:
                   e.dma_start(out=o, in_=i).then_inc(s, 16))
            return inc(sem, 16)

        def mm(out, lhsT, rhs, start, stop, tp=None, sem=None):
            def f(t, o=out, l=lhsT, r=rhs, st=start, sp=stop, tp=tp, sem=sem):
                i = t.matmul(out=o, lhsT=l, rhs=r, start=st, stop=sp,
                             tile_position=tp)
                if sem is not None:
                    i.then_inc(sems[sem], 1)
            em("tensor", f)
            if sem is not None:
                inc(sem)

        def ts(eng, out, in0, s1, op0, s2=None, op1=None, sem=None):
            def f(e, o=out, i=in0, s1=s1, s2=s2, op0=op0, op1=op1, sem=sem):
                if op1 is not None:
                    r = e.tensor_scalar(out=o, in0=i, scalar1=s1, scalar2=s2,
                                        op0=op0, op1=op1)
                else:
                    r = e.tensor_scalar(out=o, in0=i, scalar1=s1, scalar2=None,
                                        op0=op0)
                if sem is not None:
                    r.then_inc(sems[sem], 1)
            em(eng, f)
            if sem is not None:
                inc(sem)

        def tt(eng, out, in0, in1, op, sem=None):
            def f(e, o=out, a=in0, b=in1, op=op, sem=sem):
                r = e.tensor_tensor(out=o, in0=a, in1=b, op=op)
                if sem is not None:
                    r.then_inc(sems[sem], 1)
            em(eng, f)
            if sem is not None:
                inc(sem)

        def cp(eng, out, in_, sem=None):
            def f(e, o=out, i=in_, sem=sem):
                r = e.tensor_copy(out=o, in_=i)
                if sem is not None:
                    r.then_inc(sems[sem], 1)
            em(eng, f)
            if sem is not None:
                inc(sem)

        em("gpsimd", lambda g: g.memset(mask2[:], 0.0).then_inc(sems["e_stp"], 1))
        inc("e_stp")
        wt("gpsimd", "e_stp")
        em("gpsimd", lambda g: g.affine_select(
            out=mask2[:], in_=mask2[:], compare_op=ALU.is_ge, fill=NEG,
            base=-384, pattern=[[1, 512]], channel_multiplier=-1)
            .then_inc(sems["e_stp"], 1))
        inc("e_stp")
        em("gpsimd", lambda g: g.memset(onehot[:], 1.0).then_inc(sems["e_stp"], 1))
        inc("e_stp")
        wt("gpsimd", "e_stp")
        em("gpsimd", lambda g: g.affine_select(
            out=onehot[:], in_=onehot[:], compare_op=ALU.is_ge, fill=0.0,
            base=0, pattern=[[1, HPC * 64]], channel_multiplier=-64)
            .then_inc(sems["e_stp"], 1))
        inc("e_stp")
        wt("gpsimd", "e_stp")
        em("gpsimd", lambda g: g.affine_select(
            out=onehot[:], in_=onehot[:], compare_op=ALU.is_ge, fill=0.0,
            base=63, pattern=[[-1, HPC * 64]], channel_multiplier=64)
            .then_inc(sems["e_stp"], 1))
        inc("e_stp")
        em("vector", lambda v: v.memset(onesf, 1.0))
        em("vector", lambda v: v.memset(constv[:, 0:1], 1e-5))
        em("vector", lambda v: v.memset(
            VT.rearrange("p (q e) -> p q e", e=65)[:, :, 64:65], 1.0))
        em("vector", lambda v: v.engine_nop().then_inc(sems["e_stp"], 1))
        inc("e_stp")
        em("sync", lambda e: e.dma_start(
            out=bq_sb[:], in_=bqkv_h[0, :].rearrange("(m p) -> p m", p=128),
            allow_slow_non_contiguous=True).then_inc(sems["d_b"], 16))
        inc("d_b", 16)
        dma("sync", s1t[0:1, 272:784], bqkv_h[0:1, 1024:1536], "d_b")
        dma("sync", s1t[0:1, 784:1808], bout_h[0:1, :], "d_b")
        dma("sync", hselb[0:1, :], hsel_h[0:1, :], "d_b")
        wt("tensor", "d_b")
        for nn in range(4):
            mm(bank(6)[:, 128 * nn:128 * nn + 128], onesf,
               s1t[0:1, 272 + 128 * nn: 272 + 128 * nn + 128], True, True,
               sem="e_qkv")
        mm(bank(7)[:, 0:2], onesf, hselb[0:1, 0:2], True, True,
           sem="e_qkv")
        wt("vector", "e_qkv")
        em("vector", lambda v: v.tensor_copy(
            out=vbias[:, 0:520].rearrange("p (g e) -> p g e", e=65)[:, :, 0:64],
            in_=bank(6).rearrange("p (g e) -> p g e", e=64)))
        em("vector", lambda v: v.memset(
            vbias[:, 0:520].rearrange("p (g e) -> p g e", e=65)[:, :, 64:65], 0.0))
        cp("vector", hselb[:, :], bank(7)[:, 0:2], sem="e_vbc")

        def emit_ln_tile(i, src_h, is_mine):
            ri = i % 2
            bi = ri
            sb = 8 * (i % 2)
            ST = lambda k: stats[:, sb + k: sb + k + 1]
            dsem = "d_xm" if is_mine else "d_x"
            gsem = "e_xq" if is_mine else "e_xn"
            if i >= 2:
                wt("sync", gsem, cnt[gsem] - 1)
            dma("sync", xring[:, 1024 * ri: 1024 * ri + 1024],
                src_h[128 * i: 128 * i + 128, :], dsem)
            wt("scalar", dsem)
            if i >= 2:
                wt("scalar", "e_var", cnt["e_var"] - 1)
            em("scalar", lambda sc, ri=ri, ST=ST: sc.activation(
                out=xnbuf[:, 1024 * ri: 1024 * ri + 1024],
                in_=xring[:, 1024 * ri: 1024 * ri + 1024],
                func=AF.Square, accum_out=ST(0)).then_inc(sems["e_sq"], 1))
            inc("e_sq")
            wt("vector", dsem)
            em("vector", lambda v, ri=ri, ST=ST: v.reduce_sum(
                out=ST(1), in_=xring[:, 1024 * ri: 1024 * ri + 1024],
                axis=AX.X).then_inc(sems["e_vch"], 1))
            inc("e_vch")
            wt("vector", "e_vch")
            ts("vector", ST(2), ST(1), 1.0 / D, ALU.mult, sem="e_vch")
            wt("vector", "e_vch")
            tt("vector", ST(3), ST(2), ST(2), ALU.mult, sem="e_vch")
            wt("vector", "e_vch")
            wt("vector", "e_sq")
            em("vector", lambda v, ST=ST: v.scalar_tensor_tensor(
                out=ST(4), in0=ST(0), scalar=1.0 / D,
                in1=ST(3), op0=ALU.mult, op1=ALU.subtract)
                .then_inc(sems["e_var"], 1))
            inc("e_var")
            wt("scalar", "e_var")
            em("scalar", lambda sc, ST=ST: sc.activation(
                out=ST(5), in_=ST(4), func=AF.Sqrt,
                bias=constv[:, 0:1]).then_inc(sems["e_sd"], 1))
            inc("e_sd")
            wt("vector", "e_sd")
            em("vector", lambda v, ST=ST: v.reciprocal(out=ST(6), in_=ST(5))
               .then_inc(sems["e_vch"], 1))
            inc("e_vch")
            wt("vector", "e_vch")
            ts("vector", xnbuf[:, 1024 * bi: 1024 * bi + 1024],
               xring[:, 1024 * ri: 1024 * ri + 1024], ST(2),
               ALU.subtract, ST(6), ALU.mult, sem="e_vch")
            wt("vector", "e_vch")
            if not is_mine:
                cp("vector", xn16b[:, 1024 * bi: 1024 * bi + 1024],
                   xnbuf[:, 1024 * bi: 1024 * bi + 1024])
                em("vector", lambda v, bi=bi, ST=ST: v.reduce_max(
                    out=ST(7),
                    in_=xnbuf[:, 1024 * bi: 1024 * bi + 1024], axis=AX.X,
                    apply_absolute_value=True).then_inc(sems["e_vch"], 1))
                inc("e_vch")
                wt("vector", "e_vch")
                wt("vector", "e_xn")
                if i == 0:
                    cp("vector", mxt[:, 0:1], ST(7), sem="e_xn")
                else:
                    tt("vector", mxt[:, 0:1], mxt[:, 0:1], ST(7),
                       ALU.max, sem="e_xn")
                wt("sync", "e_xn")
                dma("sync", xn16_h[128 * i: 128 * i + 128, :],
                    xn16b[:, 1024 * bi: 1024 * bi + 1024], "d_xn16")
            else:
                wt("vector", "e_scl", 1)
                ts("vector", xnbuf[:, 1024 * bi: 1024 * bi + 1024],
                   xnbuf[:, 1024 * bi: 1024 * bi + 1024], scl[:, 1:2],
                   ALU.mult, MAGIC, ALU.add, sem="e_vch")
                wt("vector", "e_vch")
                ts("vector", xnbuf[:, 1024 * bi: 1024 * bi + 1024],
                   xnbuf[:, 1024 * bi: 1024 * bi + 1024], -MAGIC, ALU.add,
                   sem="e_vch")
                wt("vector", "e_vch")
                ts("vector", xq16b[:, 1024 * bi: 1024 * bi + 1024],
                   xnbuf[:, 1024 * bi: 1024 * bi + 1024], scl[:, 0:1],
                   ALU.mult, sem="e_xq")
                wt("sync", "e_xq")
                dma("sync", xq16_h[128 * i: 128 * i + 128, :],
                    xq16b[:, 1024 * bi: 1024 * bi + 1024], "d_xq")

        for i in range(NT):
            emit_ln_tile(i, x_h, False)

        for c in range(DC):
            if c >= 1:
                wt("sync", "e_cast", c)
            dma("sync", wq32[:, :], wqkv_h[128 * c: 128 * c + 128, :], "d_wq")
            wt("vector", "d_wq")
            cp("vector", wqb[:, 1536 * c: 1536 * c + 1536], wq32[:, :],
               sem="e_cast")

        qkv_group_done = [0, 0, 0, 0]
        for g in range(4):
            wt("scalar", "d_xn16", 16 * 8 * (g + 1))
            if g >= 1:
                wt("scalar", "e_qkv", qkv_group_done[g - 1])
            for c in range(DC):
                dma("scalar", xnT[:, c * 1024: c * 1024 + 1024],
                    xn16_h[1024 * g: 1024 * g + 1024, 128 * c: 128 * c + 128],
                    "d_xnT", transpose=True)
            wt("tensor", "d_xnT")
            wt("tensor", "e_cast", 8)
            for m in range(8):
                for nn in range(2):
                    bk = (m * 2 + nn) % 2
                    wt("tensor", "e_qev")
                    for c in range(DC):
                        mm(bank(bk),
                           wqb[:, 1536 * c + 128 * m: 1536 * c + 128 * m + 128],
                           xnT[:, c * 1024 + 512 * nn: c * 1024 + 512 * nn + 512],
                           c == 0, c == DC - 1,
                           sem="e_qkv" if c == DC - 1 else None)
                    wt("vector", "e_qkv")
                    if m < 4:
                        qslot = (cnt["d_qT"] // 16) % 2
                        wt("vector", "d_qT", cnt["d_qT"] - 16)
                        ts("vector", qstg[:, 512 * qslot: 512 * qslot + 512],
                           bank(bk), bq_sb[:, m:m + 1], ALU.add, sem="e_qev")
                        wt("sync", "e_qev")
                        dma("sync", qT_h[128 * m: 128 * m + 128,
                                         1024 * g + 512 * nn:
                                         1024 * g + 512 * nn + 512],
                            qstg[:, 512 * qslot: 512 * qslot + 512], "d_qT")
                    else:
                        mloc = m - 4
                        col = 4096 * mloc + 1024 * g + 512 * nn
                        ts("vector", KT[:, col: col + 512], bank(bk),
                           bq_sb[:, m:m + 1], ALU.add, sem="e_qev")
            for tl in range(8):
                tti = 8 * g + tl
                bk = 2 + tl % 2
                wt("tensor", "e_vev")
                for c in range(DC):
                    mm(bank(bk),
                       xnT[:, c * 1024 + 128 * tl: c * 1024 + 128 * tl + 128],
                       wqb[:, 1536 * c + 1024: 1536 * c + 1536],
                       c == 0, c == DC - 1,
                       sem="e_qkv" if c == DC - 1 else None)
                wt("vector", "e_qkv")
                if g == 0 and tl == 0:
                    wt("vector", "e_vbc", 1)
                vtile = VT[:, 520 * tti: 520 * tti + 520]
                tt("vector",
                   vtile.rearrange("p (gg e) -> p gg e", gg=HPC)[:, :, 0:64],
                   bank(bk).rearrange("p (gg e) -> p gg e", gg=HPC),
                   vbias[:, 0:520].rearrange("p (gg e) -> p gg e", e=65)[:, :, 0:64],
                   ALU.add, sem="e_vev")
            qkv_group_done[g] = cnt["e_qkv"]
        QKV_ALL = cnt["e_qkv"]
        VEV_ALL = cnt["e_vev"]
        QT_ALL = cnt["d_qT"]

        QEV_ALL = cnt["e_qev"]

        deq_base = cnt["e_deq"]
        wn_base = cnt["d_wn"]
        dq_base = cnt["d_wdq"]
        deq_specs = []
        for src_, dst_, n_chunks, width, wslots, col_halves in (
                (wu_h, w16_h["u"], FC, D, 2, 1),
                (wg_h, w16_h["g"], FC, D, 2, 1),
                (wd_h, w16_h["d"], 2 * DC, FF // 2, 1, 2)):
            for cchunk in range(n_chunks):
                deq_specs.append((src_, dst_, cchunk // col_halves,
                                  (cchunk % col_halves) * width, width,
                                  wslots))

        def deq_views(k):
            src_, dst_, row, c0, width, wslots = deq_specs[k]
            nb = width // 32
            rc = k % wslots
            return (src_, dst_, row, c0, width, wslots, nb,
                    dq1[:, width * rc: width * rc + width],
                    dq2[:, width * rc: width * rc + width],
                    sblk[:, 192 * rc: 192 * rc + nb],
                    sblk[:, 192 * rc + 96: 192 * rc + 96 + nb])

        def emit_deq_load(k):
            src_, dst_, row, c0, width, wslots, nb, wn, wk, sb0, sb1 = \
                deq_views(k)
            if k >= wslots:
                em("gpsimd", lambda g, v=dq_base + 16 * (k - wslots + 1):
                   g.wait_ge(sems["d_wdq"], v))
            em("gpsimd", lambda g, s=src_, row=row, c0=c0, width=width, wn=wn:
               g.dma_start(out=wn,
                           in_=s[128 * row:128 * row + 128, c0:c0 + width])
               .then_inc(sems["d_wn"], 16))

        def emit_deq_tt(k):
            src_, dst_, row, c0, width, wslots, nb, wn, wk, sb0, sb1 = \
                deq_views(k)
            rc = k % wslots
            wq_ = dq3[:, width * rc: width * rc + width]
            em("gpsimd", lambda g, v=k + 1: g.wait_ge(sems["e_dqr"], v))
            if k >= 1:
                em("gpsimd", lambda g, v=k: g.wait_ge(sems["e_dqb"], v))
            em("gpsimd", lambda g, wn=wn, wk=wk, sb1=sb1, nb=nb:
               g.tensor_tensor(
                   out=wk.rearrange("p (b e) -> p b e", e=32),
                   in0=wn.rearrange("p (b e) -> p b e", e=32),
                   in1=sb1[:, :, None].to_broadcast([128, nb, 32]),
                   op=ALU.mult).then_inc(sems["e_dqg"], 1))
            em("gpsimd", lambda g, v=k + 1: g.wait_ge(sems["e_dqb"], v))
            em("gpsimd", lambda g, wk=wk, wq_=wq_, sb0=sb0, nb=nb:
               g.tensor_tensor(
                   out=wq_.rearrange("p (b e) -> p b e", e=32),
                   in0=wk.rearrange("p (b e) -> p b e", e=32),
                   in1=sb0[:, :, None].to_broadcast([128, nb, 32]),
                   op=ALU.mult))
            em("gpsimd", lambda g, d=dst_, row=row, c0=c0, width=width,
               wq_=wq_:
               g.dma_start(out=d[128 * row:128 * row + 128, c0:c0 + width],
                           in_=wq_).then_inc(sems["d_wdq"], 16))

        def task_deq_a(k):
            src_, dst_, row, c0, width, wslots, nb, wn, wk, sb0, sb1 = \
                deq_views(k)
            em("vector", lambda v, val=wn_base + 16 * (k + 1):
               v.wait_ge(sems["d_wn"], val))
            em("vector", lambda v, wn=wn, sb0=sb0: v.tensor_reduce(
                out=sb0, in_=wn.rearrange("p (b e) -> p b e", e=32),
                axis=AX.X, op=ALU.max, apply_absolute_value=True)
               .then_inc(sems["e_vch"], 1))
            inc("e_vch")
            wt("vector", "e_vch")
            em("vector", lambda v, sb0=sb0: v.tensor_scalar(
                out=sb0, in0=sb0, scalar1=1.0 / QW, scalar2=1e-12,
                op0=ALU.mult, op1=ALU.max).then_inc(sems["e_vch"], 1))
            inc("e_vch")
            wt("vector", "e_vch")
            em("vector", lambda v, sb0=sb0, sb1=sb1: v.reciprocal(
                out=sb1, in_=sb0).then_inc(sems["e_dqr"], 1))

        def task_deq_b(k):
            src_, dst_, row, c0, width, wslots, nb, wn, wk, sb0, sb1 = \
                deq_views(k)
            em("vector", lambda v, val=k + 1:
               v.wait_ge(sems["e_dqg"], val))
            em("vector", lambda v, wk=wk: v.tensor_scalar(
                out=wk, in0=wk, scalar1=MAGIC, scalar2=-MAGIC,
                op0=ALU.add, op1=ALU.add).then_inc(sems["e_dqb"], 1))

        emit_deq_load(0)
        emit_deq_load(1)
        wt("gpsimd", "e_xn", NT)
        em("gpsimd", lambda g: g.dma_start(out=cc_i["A"].ap()[:, :],
                                           in_=mxt[:, 0:1])
           .then_inc(sems["d_ci"], 16))
        inc("d_ci", 16)
        wt("gpsimd", "d_ci")
        em("gpsimd", lambda g: g.collective_compute(
            "AllReduce", ALU.max, replica_groups=[list(range(8))],
            ins=[cc_i["A"].ap().opt()], outs=[cc_o["A"].ap().opt()])
            .then_inc(sems["c_A"], 1))
        inc("c_A")
        wt("gpsimd", "c_A")
        em("gpsimd", lambda g: g.dma_start(
            out=s1t[0:1, 0:128], in_=cc_o["A"].ap()[:, 0].unsqueeze(0))
            .then_inc(sems["d_co"], 16))
        inc("d_co", 16)
        for k in range(0, 2 * FC - 2):
            emit_deq_tt(k)
            emit_deq_load(k + 2)
        emit_deq_tt(2 * FC - 2)
        emit_deq_tt(2 * FC - 1)
        for k in range(2 * FC, 64):
            emit_deq_load(k)
            emit_deq_tt(k)
        WU16_ALL = dq_base + 16 * FC
        WG16_ALL = dq_base + 16 * 2 * FC
        WD16_ALL = dq_base + 16 * 64
        cnt["d_wdq"] = WD16_ALL
        cnt["d_wn"] = wn_base + 16 * 64
        cnt["e_dqr"] = cnt["e_dqg"] = cnt["e_dqb"] = 64

        def task_scale_A():
            wt("vector", "d_co", 16)
            em("vector", lambda v: v.reduce_max(out=s1t[0:1, 128:129],
                                                in_=s1t[0:1, 0:128],
                                                axis=AX.X)
               .then_inc(sems["e_vch"], 1))
            inc("e_vch")
            wt("vector", "e_vch")
            ts("vector", s1t[0:1, 129:130], s1t[0:1, 128:129], 1.0 / QA,
               ALU.mult, 1e-12, ALU.max, sem="e_vch")
            wt("vector", "e_vch")
            em("vector", lambda v: v.reciprocal(
                out=s1t[0:1, 130:131], in_=s1t[0:1, 129:130])
                .then_inc(sems["e_sA"], 1))
            inc("e_sA")
            wt("tensor", "e_sA", 1)
            mm(bank(6)[:, 0:2], onesf, s1t[0:1, 129:131], True, True,
               sem="e_qkv")
            wt("vector", "e_qkv")
            cp("vector", scl[:, 0:2], bank(6)[:, 0:2], sem="e_scl")

        bg_tasks = []
        for k in range(FC):
            bg_tasks.append(lambda k=k: task_deq_a(k))
            if k >= 1:
                bg_tasks.append(lambda k=k - 1: task_deq_b(k))
        bg_tasks.append(task_scale_A)
        bg_tasks.append(lambda: task_deq_b(FC - 1))
        for i in range(16):
            bg_tasks.append(lambda i=i: emit_ln_tile(i, xm_h, True))
        for k in range(FC, 2 * FC):
            bg_tasks.append(lambda k=k: task_deq_a(k))
            if k >= FC + 1:
                bg_tasks.append(lambda k=k - 1: task_deq_b(k))
        bg_tasks.append(lambda: task_deq_b(2 * FC - 1))
        for k in range(2 * FC, 64):
            bg_tasks.append(lambda k=k: task_deq_a(k))
            bg_tasks.append(lambda k=k: task_deq_b(k))

        def drain(n):
            for _ in range(min(n, len(bg_tasks))):
                bg_tasks.pop(0)()

        def emit_attention_pair(jj, dsched=None):
            for qc in range(QCH):
                qlslot = (cnt["d_q"] // 16) % 2
                if cnt["d_q"] >= 32:
                    wt("scalar", "e_sc", qsl_guard[(cnt["d_q"] // 16) - 2])
                dma("scalar", qsl[:, 512 * qlslot: 512 * qlslot + 512],
                    qT_h[128 * jj: 128 * jj + 128,
                         512 * qc: 512 * qc + 512], "d_q")
                wt("tensor", "d_q")
                for u in range(2 * qc + 2):
                    ub = u % 2
                    if u == 0:
                        wt("tensor", "e_avev")
                    scnt = [0, 0]
                    mcnt = [0, 0]
                    for dt in range(2):
                        t = 2 * u + dt
                        if cnt["e_exp"] > 0:
                            wt("tensor", "e_exp",
                               cnt["e_exp"] - (1 if dt == 0 else 0))
                        for p in range(2):
                            mm(bank(2 * dt + p),
                               KT[64 * p:64 * p + 64,
                                  4096 * jj + 128 * t:
                                  4096 * jj + 128 * t + 128],
                               qsl[64 * p:64 * p + 64,
                                   512 * qlslot: 512 * qlslot + 512],
                               True, True, tp=(64 * p, 0),
                               sem="e_sc" if p == 1 else None)
                        scnt[dt] = cnt["e_sc"]
                        r = t - 4 * qc
                        if r >= 0:
                            wt("vector", "e_sc")
                            for p in range(2):
                                tt("vector",
                                   bank(2 * dt + p)[:, 0:128 * (r + 1)],
                                   bank(2 * dt + p)[:, 0:128 * (r + 1)],
                                   mask2[:, 384 - 128 * r: 512],
                                   ALU.add, sem="e_msk" if p == 1 else None)
                        mcnt[dt] = cnt["e_msk"]
                    if cnt["e_av"] > 2:
                        wt("scalar", "e_av", cnt["e_av"] - 2)
                    pbuf = 33024 + ub * 2048
                    for dt in range(2):
                        wt("scalar", "e_sc", scnt[dt])
                        wt("scalar", "e_msk", mcnt[dt])
                        PTh = A3[:, pbuf + 1024 * dt: pbuf + 1024 * dt + 1024]
                        em("scalar", lambda sc, dt=dt, PTh=PTh:
                           sc.activation(
                               out=PTh, in_=bank(2 * dt, 2), func=AF.Exp,
                               scale=0.125).then_inc(sems["e_exp"], 1))
                        inc("e_exp")
                    wt("tensor", "e_exp")
                    for p in range(2):
                        gloc = 2 * jj + p
                        for dt in range(2):
                            t = 2 * u + dt
                            mm(PS[0:65, 512 * (4 + p): 512 * (4 + p) + 512],
                               VT[:, 520 * t + 65 * gloc:
                                  520 * t + 65 * gloc + 65],
                               A3[:, pbuf + 1024 * dt + 512 * p:
                                  pbuf + 1024 * dt + 512 * p + 512],
                               t == 0, t == 4 * qc + 3,
                               sem="e_av" if dt == 1 else None)
                for p in range(2):
                    gloc = 2 * jj + p
                    ks = (cnt["d_ans"] // 16) % 2
                    wt("vector", "e_av")
                    if cnt["d_ans"] >= 32:
                        wt("vector", "d_ans", cnt["d_ans"] - 16)
                    if cnt["d_l"] >= 32:
                        wt("vector", "d_l", cnt["d_l"] - 16)
                    cp("vector", ansst[0:65, 512 * ks: 512 * ks + 512],
                       PS[0:65, 512 * (4 + p): 512 * (4 + p) + 512])
                    cp("vector", lstg[0:65, 512 * ks: 512 * ks + 512],
                       PS[0:65, 512 * (4 + p): 512 * (4 + p) + 512],
                       sem="e_avev")
                    wt("scalar", "e_avev")
                    dma("scalar", ansr_h[64 * gloc: 64 * gloc + 64,
                                         512 * qc: 512 * qc + 512],
                        ansst[0:64, 512 * ks: 512 * ks + 512], "d_ans")
                    dma("scalar", lbuf_h[0:1, 4096 * gloc + 512 * qc:
                                         4096 * gloc + 512 * qc + 512],
                        lstg[64:65, 512 * ks: 512 * ks + 512], "d_l")
                qsl_guard.append(cnt["e_sc"])
                drain(dsched[qc] if dsched is not None else qc + 3)

        qsl_guard = []

        def emit_ffn_mm(which):
            out_hbm = fup_h if which == "u" else fga_h
            wtsem = "d_WTu" if which == "u" else "d_WTg"
            dsem = "d_fup" if which == "u" else "d_fga"
            mcol = 1 if which == "u" else 2
            wt("tensor", wtsem, cnt[wtsem])
            wt("tensor", "d_xqT", cnt["d_xqT"])
            wt("tensor", "e_scl", 1)
            for m in range(FC):
                for qc4 in range(4):
                    bk = 6 + (m * 4 + qc4) % 2
                    wt("tensor", "e_fev", cnt["e_fev"] - 1)
                    for c in range(DC):
                        mm(bank(bk),
                           WT[:, 3072 * c + 128 * m: 3072 * c + 128 * m + 128],
                           xqT[:, TOK * c + 512 * qc4:
                               TOK * c + 512 * qc4 + 512],
                           c == 0, c == DC - 1,
                           sem="e_upm" if c == DC - 1 else None)
                    wt("vector", "e_upm")
                    wt("vector", dsem, cnt[dsem] - 16 * 3)
                    if qc4 == 0:
                        wt("vector", "e_fmx", cnt["e_fmx"] - 1)
                    cp("vector", fst[:, 512 * qc4: 512 * qc4 + 512],
                       bank(bk), sem="e_fev")
                    wt("sync", "e_fev")
                    dma("sync", out_hbm[128 * m:128 * m + 128,
                                        512 * qc4: 512 * qc4 + 512],
                        fst[:, 512 * qc4: 512 * qc4 + 512], dsem)
                wt("vector", "e_fev")
                em("vector", lambda v: v.tensor_reduce(
                    out=mxt[:, 4:8],
                    in_=fst[:, 0:2048].rearrange("p (b e) -> p b e", b=4),
                    axis=AX.X, op=ALU.max, apply_absolute_value=True)
                   .then_inc(sems["e_vch"], 1))
                inc("e_vch")
                wt("vector", "e_vch")
                em("vector", lambda v: v.tensor_reduce(
                    out=mxt[:, 3:4], in_=mxt[:, 4:8], axis=AX.X,
                    op=ALU.max).then_inc(sems["e_vch"], 1))
                inc("e_vch")
                wt("vector", "e_vch")
                wt("vector", "e_fmx")
                if m == 0:
                    cp("vector", mxt[:, mcol:mcol + 1], mxt[:, 3:4],
                       sem="e_fmx")
                else:
                    tt("vector", mxt[:, mcol:mcol + 1], mxt[:, mcol:mcol + 1],
                       mxt[:, 3:4], ALU.max, sem="e_fmx")

        wt("scalar", "d_qT", QT_ALL)
        wt("tensor", "e_qev", QEV_ALL)
        wt("tensor", "e_vev", VEV_ALL)
        wt("tensor", "e_qkv", QKV_ALL)
        wt("tensor", "e_vbc", 1)
        wt("vector", "e_stp", cnt["e_stp"])
        emit_attention_pair(0)
        emit_attention_pair(1)
        wt("sync", "e_qkv", QKV_ALL)
        wt("sync", "e_cast", 8)
        wt("sync", "d_wdq", WU16_ALL)
        for c in range(DC):
            dma("sync", WT[:, 3072 * c: 3072 * c + 3072],
                w16_h["u"][:, 128 * c: 128 * c + 128], "d_WTu",
                transpose=True)
        wt("sync", "d_xq", 16 * 16)
        wt("sync", "e_xq", cnt["e_xq"])
        for c in range(DC):
            dma("sync", xqT[:, TOK * c: TOK * c + TOK],
                xq16_h[:, 128 * c: 128 * c + 128], "d_xqT", transpose=True)
        emit_ffn_mm("u")
        UPM_ALL = cnt["e_upm"]
        emit_attention_pair(2)
        wt("sync", "d_wdq", WG16_ALL)
        wt("sync", "e_upm", UPM_ALL)
        for c in range(DC):
            dma("sync", WT[:, 3072 * c: 3072 * c + 3072],
                w16_h["g"][:, 128 * c: 128 * c + 128], "d_WTg",
                transpose=True)
        emit_ffn_mm("g")
        em("tensor", lambda t: t.nop().then_inc(sems["e_gmm"], 1))
        inc("e_gmm")
        FEV_ALL = cnt["e_fev"]

        def cc_round(key, incol, n):
            em("gpsimd", lambda g, key=key, incol=incol, n=n: g.dma_start(
                out=cc_i[key].ap()[:, :], in_=mxt[:, incol:incol + n])
                .then_inc(sems["d_ci"], 16))
            inc("d_ci", 16)
            wt("gpsimd", "d_ci")
            em("gpsimd", lambda g, key=key: g.collective_compute(
                "AllReduce", ALU.max, replica_groups=[list(range(8))],
                ins=[cc_i[key].ap().opt()], outs=[cc_o[key].ap().opt()])
                .then_inc(sems["c_" + key], 1))
            inc("c_" + key)
            wt("gpsimd", "c_" + key)
            em("gpsimd", lambda g, key=key, n=n: g.dma_start(
                out=s1t[0:1, 0:128 * n].rearrange("o (k p) -> o k p", k=n),
                in_=cc_o[key].ap()[:, :].rearrange("p k -> k p").unsqueeze(0),
                allow_slow_non_contiguous=True).then_inc(sems["d_co"], 16))
            inc("d_co", 16)

        def scale_calc(n_src, scol, esem):
            wt("vector", "d_co", cnt["d_co"])
            for k in range(n_src):
                em("vector", lambda v, k=k, n=n_src: v.reduce_max(
                    out=s1t[0:1, 128 * n + k: 128 * n + k + 1],
                    in_=s1t[0:1, 128 * k: 128 * k + 128], axis=AX.X)
                   .then_inc(sems["e_vch"], 1))
                inc("e_vch")
                wt("vector", "e_vch")
                ts("vector", s1t[0:1, 260 + 2 * k: 261 + 2 * k],
                   s1t[0:1, 128 * n_src + k: 128 * n_src + k + 1],
                   1.0 / QA, ALU.mult, 1e-12, ALU.max, sem="e_vch")
                wt("vector", "e_vch")
                em("vector", lambda v, k=k: v.reciprocal(
                    out=s1t[0:1, 261 + 2 * k: 262 + 2 * k],
                    in_=s1t[0:1, 260 + 2 * k: 261 + 2 * k]))
            em("vector", lambda v: v.engine_nop().then_inc(sems[esem], 1))
            inc(esem)
            wt("tensor", esem, cnt[esem])
            mm(bank(6)[:, 0:2 * n_src], onesf,
               s1t[0:1, 260: 260 + 2 * n_src], True, True, sem="e_qkv")
            wt("vector", "e_qkv")
            cp("vector", scl[:, scol: scol + 2 * n_src],
               bank(6)[:, 0:2 * n_src], sem="e_scl")


        def task_pass2(m):
            rm = m % 2
            eng = "vector"
            csem = "e_vch"
            einv = ein[:, 2048 * rm: 2048 * rm + 2048]
            ein2v = ein2[:, 2048 * rm: 2048 * rm + 2048]
            wt("sync", "c_B", 1)
            if m <= 1:
                wt("sync", "e_gmm", 1)
                wt("sync", "e_fev", FEV_ALL)
            if cnt["d_fg2"] >= 32:
                wt("sync", "d_fg2", cnt["d_fg2"] - 16)
            dma("sync", einv, fga_h[128 * m: 128 * m + 128, :], "d_p2")
            wt(eng, "d_p2")
            wt(eng, "e_scl", 2)
            em(eng, lambda e, einv=einv, s=sems[csem]: e.tensor_scalar(
                out=einv, in0=einv, scalar1=scl[:, 5:6], scalar2=MAGIC,
                op0=ALU.mult, op1=ALU.add).then_inc(s, 1))
            inc(csem)
            wt(eng, csem)
            em(eng, lambda e, einv=einv: e.tensor_scalar(
                out=einv, in0=einv, scalar1=-MAGIC, scalar2=scl[:, 4:5],
                op0=ALU.add, op1=ALU.mult).then_inc(sems["e_ew"], 1))
            inc("e_ew")
            wt("scalar", "e_ew")
            em("scalar", lambda sc, einv=einv, ein2v=ein2v: sc.activation(
                out=ein2v, in_=einv, func=AF.Sigmoid)
                .then_inc(sems["e_sig"], 1))
            inc("e_sig")
            wt(eng, "e_sig")
            em(eng, lambda e, ein2v=ein2v, s=sems[csem]: e.tensor_scalar(
                out=ein2v, in0=ein2v, scalar1=U16, scalar2=MAGIC,
                op0=ALU.mult, op1=ALU.add).then_inc(s, 1))
            inc(csem)
            wt(eng, csem)
            em(eng, lambda e, ein2v=ein2v, s=sems[csem]: e.tensor_scalar(
                out=ein2v, in0=ein2v, scalar1=-MAGIC, scalar2=65535.0,
                op0=ALU.add, op1=ALU.min).then_inc(s, 1))
            inc(csem)
            wt(eng, csem)
            em(eng, lambda e, ein2v=ein2v, s=sems[csem]: e.tensor_scalar(
                out=ein2v, in0=ein2v, scalar1=0.0, scalar2=1.0 / U16,
                op0=ALU.max, op1=ALU.mult).then_inc(s, 1))
            inc(csem)
            wt(eng, csem)
            em(eng, lambda e, einv=einv, ein2v=ein2v: e.tensor_tensor(
                out=einv, in0=einv, in1=ein2v, op=ALU.mult)
                .then_inc(sems["e_ew"], 1))
            inc("e_ew")
            wt("vector", "e_ew")
            em("vector", lambda v, einv=einv: v.tensor_reduce(
                out=mxt[:, 8:9], in_=einv.rearrange("p (b e) -> p b e", b=1),
                axis=AX.X, op=ALU.max, apply_absolute_value=True)
               .then_inc(sems["e_vch"], 1))
            inc("e_vch")
            wt("vector", "e_vch")
            if m == 0:
                cp("vector", mxt[:, 3:4], mxt[:, 8:9], sem="e_ew")
            else:
                wt("vector", "e_ew")
                tt("vector", mxt[:, 3:4], mxt[:, 3:4], mxt[:, 8:9], ALU.max,
                   sem="e_ew")
            wt("sync", "e_ew")
            dma("sync", fg2_h[128 * m: 128 * m + 128, :], einv, "d_fg2")


        wt("gpsimd", "e_fmx", 2 * FC)
        cc_round("B", 1, 2)
        bg_tasks.append(lambda: scale_calc(2, 2, "e_sB"))
        for m in range(FC):
            bg_tasks.append(lambda m=m: task_pass2(m))
        emit_attention_pair(3, dsched=(0, 0, 0, 0, 7, 6, 6, 6))
        drain(len(bg_tasks))
        SC_ALL = cnt["e_sc"]
        ANS_ALL = cnt["d_ans"]
        L_ALL = cnt["d_l"]
        AVEV_ALL = cnt["e_avev"]

        wt("scalar", "d_l", L_ALL)
        wt("scalar", "e_sc", SC_ALL)
        dma("scalar", l8[0:HPC, :],
            lbuf_h[0, :].rearrange("(g n) -> g n", g=HPC), "d_l8")
        wt("vector", "d_l8", 16)
        em("vector", lambda v: v.reciprocal(out=l8[0:HPC, :],
                                            in_=l8[0:HPC, :])
           .then_inc(sems["e_il"], 1))
        inc("e_il")
        for gl in range(HPC):
            wt("scalar", "d_exi", cnt["d_exi"])
            wt("scalar", "d_ans", ANS_ALL)
            if gl == 0:
                wt("scalar", "e_avev", AVEV_ALL)
            dma("scalar", anr[0:64, :], ansr_h[64 * gl: 64 * gl + 64, :],
                "d_abl")
            wt("tensor", "e_il", 1)
            for qq in range(4):
                bkq = qq % 2
                wt("tensor", "e_nrm", cnt["e_nrm"] - 1)
                for jh in range(2):
                    mm(PS[0:64,
                          512 * (2 * bkq + jh): 512 * (2 * bkq + jh) + 512],
                       onehot[0:HPC, 64 * gl: 64 * gl + 64],
                       l8[0:HPC, 1024 * qq + 512 * jh:
                          1024 * qq + 512 * jh + 512],
                       True, True, sem="e_bb" if jh == 1 else None)
                wt("vector", "e_bb")
                wt("vector", "d_abl")
                tt("vector", anr[0:64, 1024 * qq: 1024 * qq + 1024],
                   anr[0:64, 1024 * qq: 1024 * qq + 1024],
                   PS[0:64, 1024 * bkq: 1024 * bkq + 1024], ALU.mult,
                   sem="e_nrm")
            wt("scalar", "e_nrm")
            dma("scalar", exi_h.ap()[64 * gl: 64 * gl + 64, :],
                anr[0:64, :], "d_exi")

        wt("gpsimd", "e_ew", 3 * FC)
        cc_round("C", 3, 1)
        scale_calc(1, 6, "e_sC")

        for m in range(FC):
            rm = m % 2
            eng = "vector"
            einv = ein[:, 2048 * rm: 2048 * rm + 2048]
            ein2v = ein2[:, 2048 * rm: 2048 * rm + 2048]
            wt("sync", "c_C", 1)
            wt("sync", "e_p3", cnt["e_p3"] - 1)
            dma("sync", einv, fg2_h[128 * m: 128 * m + 128, :], "d_p3")
            dma("sync", ein2v, fup_h[128 * m: 128 * m + 128, :], "d_p3")
            wt(eng, "d_p3")
            wt(eng, "e_scl", 3)
            csem = "e_gch" if eng == "gpsimd" else "e_vch"
            for s1a, s2a, o0, o1, tgt in (
                    (scl[:, 7:8], MAGIC, ALU.mult, ALU.add, einv),
                    (-MAGIC, scl[:, 6:7], ALU.add, ALU.mult, einv),
                    (scl[:, 3:4], MAGIC, ALU.mult, ALU.add, ein2v),
                    (-MAGIC, scl[:, 2:3], ALU.add, ALU.mult, ein2v)):
                em(eng, lambda e, t=tgt, s1a=s1a, s2a=s2a, o0=o0, o1=o1,
                   s=sems[csem]:
                   e.tensor_scalar(out=t, in0=t, scalar1=s1a, scalar2=s2a,
                                   op0=o0, op1=o1).then_inc(s, 1))
                inc(csem)
                wt(eng, csem)
            if eng == "gpsimd":
                em(eng, lambda e, einv=einv, ein2v=ein2v: e.tensor_tensor(
                    out=einv, in0=einv, in1=ein2v, op=ALU.mult)
                    .then_inc(sems["e_p3g"], 1))
                inc("e_p3g")
                wt("vector", "e_p3g")
            else:
                em(eng, lambda e, einv=einv, ein2v=ein2v: e.tensor_tensor(
                    out=einv, in0=einv, in1=ein2v, op=ALU.mult)
                    .then_inc(sems["e_vch"], 1))
                inc("e_vch")
                wt("vector", "e_vch")
            em("vector", lambda v, einv=einv: v.tensor_reduce(
                out=mxt[:, 8:9],
                in_=einv.rearrange("p (b e) -> p b e", b=1),
                axis=AX.X, op=ALU.max, apply_absolute_value=True)
               .then_inc(sems["e_vch"], 1))
            inc("e_vch")
            wt("vector", "e_vch")
            if m == 0:
                cp("vector", mxt[:, 4:5], mxt[:, 8:9], sem="e_p3")
            else:
                tt("vector", mxt[:, 4:5], mxt[:, 4:5], mxt[:, 8:9], ALU.max,
                   sem="e_p3")

        wt("gpsimd", "e_p3", FC)
        cc_round("D", 4, 1)
        scale_calc(1, 8, "e_sD")

        for m in range(FC):
            rm = m % 2
            eng = "vector"
            einv = ein[:, 2048 * rm: 2048 * rm + 2048]
            ein2v = ein2[:, 2048 * rm: 2048 * rm + 2048]
            eo16v = eo16[:, 2048 * rm: 2048 * rm + 2048]
            wt("sync", "c_D", 1)
            wt("sync", "e_p4", cnt["e_p4"] - 1)
            dma("sync", einv, fg2_h[128 * m: 128 * m + 128, :], "d_p4")
            dma("sync", ein2v, fup_h[128 * m: 128 * m + 128, :], "d_p4")
            wt(eng, "d_p4")
            wt(eng, "e_scl", 4)
            if m <= 1:
                wt(eng, "e_fev", FEV_ALL)
            csem = "e_gch" if eng == "gpsimd" else "e_vch"
            for s1a, s2a, o0, o1, tgt in (
                    (scl[:, 7:8], MAGIC, ALU.mult, ALU.add, einv),
                    (-MAGIC, scl[:, 6:7], ALU.add, ALU.mult, einv),
                    (scl[:, 3:4], MAGIC, ALU.mult, ALU.add, ein2v),
                    (-MAGIC, scl[:, 2:3], ALU.add, ALU.mult, ein2v)):
                em(eng, lambda e, t=tgt, s1a=s1a, s2a=s2a, o0=o0, o1=o1,
                   s=sems[csem]:
                   e.tensor_scalar(out=t, in0=t, scalar1=s1a, scalar2=s2a,
                                   op0=o0, op1=o1).then_inc(s, 1))
                inc(csem)
                wt(eng, csem)
            em(eng, lambda e, einv=einv, ein2v=ein2v, s=sems[csem]:
               e.tensor_tensor(
                   out=einv, in0=einv, in1=ein2v, op=ALU.mult).then_inc(s, 1))
            inc(csem)
            wt(eng, csem)
            em(eng, lambda e, einv=einv, s=sems[csem]: e.tensor_scalar(
                out=einv, in0=einv, scalar1=scl[:, 9:10], scalar2=MAGIC,
                op0=ALU.mult, op1=ALU.add).then_inc(s, 1))
            inc(csem)
            wt(eng, csem)
            wt(eng, "d_foq", cnt["d_foq"] - 16)
            em(eng, lambda e, einv=einv, eo16v=eo16v: e.tensor_scalar(
                out=eo16v, in0=einv, scalar1=-MAGIC, scalar2=scl[:, 8:9],
                op0=ALU.add, op1=ALU.mult).then_inc(sems["e_p4"], 1))
            inc("e_p4")
            wt("sync", "e_p4")
            dma("sync", foq_h[128 * m: 128 * m + 128, :], eo16v, "d_foq")
        FOQ_ALL = cnt["d_foq"]
        P4_ALL = cnt["e_p4"]

        wt("gpsimd", "d_exi", cnt["d_exi"])
        em("gpsimd", lambda g: g.collective_compute(
            "AllGather", ALU.bypass,
            replica_groups=[[0, 1], [2, 3], [4, 5], [6, 7]],
            ins=[exi_h.ap().opt()], outs=[exo_h.ap().opt()])
            .then_inc(sems["c_AG"], 1))
        inc("c_AG")
        wt("scalar", "c_AG", 1)
        for gg in range(NH):
            wt("scalar", "d_am", cnt["d_am"])
            dma("scalar", blh[0:64, 0:2048],
                exo_h.ap()[64 * gg: 64 * gg + 64, 0:TOK], "d_abl")
            dma("scalar", blh[0:64, 2048:4096],
                exo_h.ap()[64 * gg: 64 * gg + 64, TOK: 2 * TOK], "d_abl")
            wt("vector", "d_abl")
            wt("vector", "d_am", cnt["d_am"])
            ts("vector", blst[0:64, :], blh[0:64, 0:2048],
               hselb[0:64, 0:1], ALU.mult, sem="e_vch")
            wt("vector", "e_vch")
            em("vector", lambda v: v.scalar_tensor_tensor(
                out=blst[0:64, :], in0=blh[0:64, 2048:4096],
                scalar=hselb[0:64, 1:2], in1=blst[0:64, :],
                op0=ALU.mult, op1=ALU.add).then_inc(sems["e_bl"], 1))
            inc("e_bl")
            wt("scalar", "e_bl")
            dma("scalar", ansm_h[64 * gg: 64 * gg + 64, :], blst[0:64, :],
                "d_am")
        BL_ALL = cnt["e_bl"]
        AM_ALL = cnt["d_am"]

        wt("sync", "d_wdq", WD16_ALL)
        wt("sync", "e_p4", P4_ALL)
        wt("sync", "e_bl", BL_ALL)
        for k in range(2 * NH):
            if k >= 1:
                wt("sync", "e_woc", cnt["e_woc"])
            dma("sync", wost[0:64, 0:512],
                wout_h[0:64, 512 * k: 512 * k + 512], "d_wo")
            wt("vector", "d_wo")
            cp("vector", wosb[0:64, 512 * k: 512 * k + 512],
               wost[0:64, 0:512], sem="e_woc")
        WOC_ALL = cnt["e_woc"]

        wt("scalar", "d_wdq", WD16_ALL)
        wt("scalar", "e_gmm", 1)
        for c in range(FC):
            dma("scalar", WT[:, 1024 * c: 1024 * c + 1024],
                w16_h["d"][:, 128 * c: 128 * c + 128], "d_WTd", transpose=True)

        wt("tensor", "e_nrm", cnt["e_nrm"])
        for nn in range(2):
            mm(bank(nn), onesf,
               s1t[0:1, 784 + 512 * nn: 784 + 512 * nn + 512], True, True,
               sem="e_qkv")
        wt("vector", "e_qkv")
        wt("vector", "d_wdq", WD16_ALL)
        cp("vector", boutbc[:, 0:512], bank(0))
        cp("vector", boutbc[:, 512:1024], bank(1), sem="e_bb")
        BB_ALL = cnt["e_bb"]

        for tti in range(16):
            rt = tti % 2
            wt("sync", "e_dp", cnt["e_dp"] - 2)
            wt("sync", "d_foq", FOQ_ALL)
            dma("sync",
                oqsl[:, 3072 * rt: 3072 * rt + 3072]
                .rearrange("p (c n) -> p c n", c=FC),
                foq_h[:, 128 * tti: 128 * tti + 128]
                .rearrange("(c p) n -> p c n", p=128), "d_oq")
            wt("sync", "e_fin", cnt["e_fin"])
            dma("sync", xfin[:, :], xm_h[128 * tti: 128 * tti + 128, :], "d_xf")
            wt("scalar", "d_am", AM_ALL)
            wt("scalar", "e_op", cnt["e_op"] - 2)
            dma("scalar",
                ansl[0:64, 2048 * rt: 2048 * rt + 2048]
                .rearrange("p (g n) -> p g n", g=NH),
                ansm_h[:, 128 * tti: 128 * tti + 128]
                .rearrange("(g p) n -> p g n", p=64), "d_q")
            wt("tensor", "d_oq")
            wt("tensor", "d_WTd", 16 * FC)
            wt("tensor", "e_bb", BB_ALL)
            wt("tensor", "e_fin", cnt["e_fin"] - 1)
            for nn in range(2):
                for c in range(FC):
                    mm(bank(2 * rt + nn),
                       oqsl[:, 3072 * rt + 128 * c: 3072 * rt + 128 * c + 128],
                       WT[:, 1024 * c + 512 * nn: 1024 * c + 512 * nn + 512],
                       c == 0, c == FC - 1,
                       sem="e_dp" if c == FC - 1 else None)
            wt("tensor", "d_q")
            wt("tensor", "e_woc", WOC_ALL)
            for nn in range(2):
                for gg in range(NH):
                    mm(bank(4 + 2 * rt + nn),
                       ansl[0:64, 2048 * rt + 128 * gg:
                            2048 * rt + 128 * gg + 128],
                       wosb[0:64, 1024 * gg + 512 * nn:
                            1024 * gg + 512 * nn + 512],
                       gg == 0, gg == NH - 1,
                       sem="e_op" if gg == NH - 1 else None)
            wt("vector", "d_xf")
            wt("vector", "e_dp")
            wt("vector", "e_op")
            if tti == 0:
                wt("vector", "e_woc", WOC_ALL)
            for nn in range(2):
                t1v = t1b[:, 1024 * rt + 512 * nn: 1024 * rt + 512 * nn + 512]
                tt("vector", t1v, bank(4 + 2 * rt + nn),
                   xfin[:, 512 * nn: 512 * nn + 512], ALU.add, sem="e_vch")
                wt("vector", "e_vch")
                tt("vector", t1v, t1v, bank(2 * rt + nn), ALU.add, sem="e_vch")
                wt("vector", "e_vch")
                tt("vector", t1v, t1v, boutbc[:, 512 * nn: 512 * nn + 512],
                   ALU.add, sem="e_fin" if nn == 1 else None)
            wt("sync", "e_fin")
            dma("sync", out_h[128 * tti: 128 * tti + 128, :],
                t1b[:, 1024 * rt: 1024 * rt + 1024], "d_out")

        wt("sync", "d_out", 16 * 16)
        if dbg:
            for nm, srct in (("dxq", xq16_h), ("dwu", w16_h["u"]),
                             ("dfga", fga_h), ("dfg2", fg2_h),
                             ("dfoq", foq_h), ("dfup", fup_h)):
                dma("sync", dbg_h[nm][:, :], srct[:, :], "d_b")
            wt("sync", "d_b", cnt["d_b"])

        with nc.Block() as block:

            @block.sync
            def _(eng):
                for f in prog["sync"]:
                    f(eng)

            @block.scalar
            def _(eng):
                for f in prog["scalar"]:
                    f(eng)

            @block.vector
            def _(eng):
                for f in prog["vector"]:
                    f(eng)

            @block.tensor
            def _(eng):
                for f in prog["tensor"]:
                    f(eng)

            @block.gpsimd
            def _(eng):
                for f in prog["gpsimd"]:
                    f(eng)

    return nc


_NC_CACHE = None


def _get_nc():
    global _NC_CACHE
    if _NC_CACHE is None:
        _NC_CACHE = build_nc()
    return _NC_CACHE


def kernel(L0_attn_x, L0_attn_w_qkv, L0_attn_b_qkv, L0_attn_w_out, L0_attn_b_out,
           L0_ffn_w_gate, L0_ffn_w_up, L0_ffn_w_down):
    x = np.ascontiguousarray(np.asarray(L0_attn_x, dtype=np.float32))
    wqkv = np.asarray(L0_attn_w_qkv, dtype=np.float32)
    bqkv = np.asarray(L0_attn_b_qkv, dtype=np.float32)
    wout = np.asarray(L0_attn_w_out, dtype=np.float32)
    bout = np.asarray(L0_attn_b_out, dtype=np.float32)
    wg = np.ascontiguousarray(np.asarray(L0_ffn_w_gate, dtype=np.float32))
    wu = np.ascontiguousarray(np.asarray(L0_ffn_w_up, dtype=np.float32))
    wd = np.ascontiguousarray(np.asarray(L0_ffn_w_down, dtype=np.float32))

    wo_arr = np.ascontiguousarray(wout.transpose(1, 0, 2).reshape(64, NH * D))
    bout_r = np.ascontiguousarray(bout.reshape(1, D))

    nc = _get_nc()
    in_maps = []
    for c in range(8):
        b, h = c // 2, c % 2
        heads = slice(8 * h, 8 * h + 8)
        wq_c = np.ascontiguousarray(wqkv[:, :, heads, :].reshape(D, 1536))
        bq_c = np.ascontiguousarray(bqkv[:, heads, :].reshape(1, 1536))
        xb = np.ascontiguousarray(x[b])
        xm = np.ascontiguousarray(x[b, 2048 * h: 2048 * h + 2048])
        hsel = np.array([[1.0 - h, float(h)]], dtype=np.float32)
        in_maps.append({
            "x": xb, "xmine": xm, "wqkv": wq_c, "bqkv": bq_c,
            "wout": wo_arr, "bout": bout_r, "hsel": hsel,
            "wg": wg, "wu": wu, "wd": wd,
        })
    res = run_bass_kernel_spmd(nc, in_maps, core_ids=list(range(8)))
    out = np.empty((B, S, D), dtype=np.float32)
    for c in range(8):
        b, h = c // 2, c % 2
        out[b, 2048 * h: 2048 * h + 2048] = res.results[c]["out"]
    return out


# version 10 (final)
# speedup vs baseline: 1.1681x; 1.1681x over previous
import numpy as np
from contextlib import ExitStack

import concourse.bass as bass
import concourse.mybir as mybir
from concourse.bass_utils import run_bass_kernel_spmd

F32 = mybir.dt.float32
BF16 = mybir.dt.bfloat16
AF = mybir.ActivationFunctionType
ALU = mybir.AluOpType
AX = mybir.AxisListType

B, S, D, NH, DH = 4, 4096, 1024, 16, 64
FF = 3072
HPC, TOK, NT, DC, FC = 8, 2048, 32, 8, 24
QCH = 8
MAGIC = 12582912.0
QA, QW = 32767.0, 7.0
NEG = -1.0e9
U16 = 65536.0


def build_nc(dbg=False):
    nc = bass.Bass("TRN2", target_bir_lowering=False, debug=False,
                   detect_race_conditions=False)

    x_h = nc.dram_tensor("x", [S, D], F32, kind="ExternalInput").ap()
    xm_h = nc.dram_tensor("xmine", [TOK, D], F32, kind="ExternalInput").ap()
    wqkv_h = nc.dram_tensor("wqkv", [D, 1536], F32, kind="ExternalInput").ap()
    bqkv_h = nc.dram_tensor("bqkv", [1, 1536], F32, kind="ExternalInput").ap()
    wout_h = nc.dram_tensor("wout", [64, NH * D], F32, kind="ExternalInput").ap()
    bout_h = nc.dram_tensor("bout", [1, D], F32, kind="ExternalInput").ap()
    hsel_h = nc.dram_tensor("hsel", [1, 2], F32, kind="ExternalInput").ap()
    wg_h = nc.dram_tensor("wg", [FF, D], F32, kind="ExternalInput").ap()
    wu_h = nc.dram_tensor("wu", [FF, D], F32, kind="ExternalInput").ap()
    wd_h = nc.dram_tensor("wd", [D, FF], F32, kind="ExternalInput").ap()
    out_h = nc.dram_tensor("out", [TOK, D], F32, kind="ExternalOutput").ap()

    xn16_h = nc.dram_tensor("xn16", [S, D], BF16).ap()
    xq16_h = nc.dram_tensor("xq16", [TOK, D], BF16).ap()
    qT_h = nc.dram_tensor("qTs", [4 * 128, S], BF16).ap()
    w16_h = {"u": nc.dram_tensor("wu16", [FF, D], BF16).ap(),
             "g": nc.dram_tensor("wg16", [FF, D], BF16).ap(),
             "d": nc.dram_tensor("wd16", [D, FF], BF16).ap()}
    fup_h = nc.dram_tensor("fup", [FF, TOK], F32).ap()
    fga_h = nc.dram_tensor("fgate", [FF, TOK], F32).ap()
    fg2_h = nc.dram_tensor("fg2", [FF, TOK], F32).ap()
    foq_h = nc.dram_tensor("foq", [FF, TOK], BF16).ap()
    ansr_h = nc.dram_tensor("ansraw", [HPC * DH, S], BF16).ap()
    lbuf_h = nc.dram_tensor("lbuf", [1, HPC * S], F32).ap()
    ansm_h = nc.dram_tensor("ansmine", [NH * DH, TOK], BF16).ap()
    cc_i = {k: nc.dram_tensor(f"cc{k}_i", [128, 2 if k == "B" else 1], F32)
            for k in "ABCD"}
    cc_o = {k: nc.dram_tensor(f"cc{k}_o", [128, 2 if k == "B" else 1], F32)
            for k in "ABCD"}
    exi_h = nc.dram_tensor("exch_i", [HPC * DH, S], BF16)
    exo_h = nc.dram_tensor("exch_o", [NH * DH, S], BF16)
    dbg_h = {}
    if dbg:
        for nm, shp, dty in (("dxq", [TOK, D], BF16), ("dwu", [FF, D], BF16),
                             ("dfga", [FF, TOK], F32), ("dfg2", [FF, TOK], F32),
                             ("dfoq", [FF, TOK], BF16), ("dfup", [FF, TOK], F32)):
            dbg_h[nm] = nc.dram_tensor(nm, shp, dty, kind="ExternalOutput").ap()

    with ExitStack() as ctx:
        E = ctx.enter_context

        s1t = E(nc.sbuf_tensor("s1t", [1, 1936], F32))
        onesf = s1t[0:1, 1808:1936]
        bq_sb = E(nc.sbuf_tensor("bq_sb", [128, 12], F32))
        scl = E(nc.sbuf_tensor("scl", [128, 16], F32))
        hselb = E(nc.sbuf_tensor("hselb", [128, 2], F32))
        stats = E(nc.sbuf_tensor("stats", [128, 16], F32))
        mxt = E(nc.sbuf_tensor("mxt", [128, 16], F32))
        onehot = E(nc.sbuf_tensor("onehot", [HPC, HPC * 64], F32))
        constv = E(nc.sbuf_tensor("constv", [128, 2], F32))

        A1 = E(nc.sbuf_tensor("A1", [128, 24576], BF16))
        A2 = E(nc.sbuf_tensor("A2", [128, 20480], BF16))
        A3 = E(nc.sbuf_tensor("A3", [128, 42240], BF16))
        A4 = E(nc.sbuf_tensor("A4", [128, 12288], BF16))
        PS = E(nc.psum_tensor("PS", [128, 4096], F32))

        def bank(i, n=1):
            return PS[:, 512 * i: 512 * (i + n)]

        wqb = A1[:, 0:12288]
        xnT = A1[:, 12288:20480]
        wq32 = A1[:, 20480:23552].bitcast(F32)
        WT = A1[:, 0:24576]
        xring = A2[:, 0:4096].bitcast(F32)
        xnbuf = A2[:, 4096:8192].bitcast(F32)
        xn16b = A2[:, 8192:10240]
        xq16b = A2[:, 10240:12288]
        qstg = A2[:, 12288:13312]
        vbias = A2[:, 13312:14352].bitcast(F32)
        sblk = A4[:, 10240:12288].bitcast(F32)
        xqT = A2[:, 0:16384]
        fst = A2[:, 16384:20480].bitcast(F32)
        ein = A2[:, 0:8192].bitcast(F32)
        ein2 = A2[:, 8192:16384].bitcast(F32)
        eo16 = A2[:, 16384:20480]
        KT = A3[:, 0:16384]
        VT = A3[:, 16384:33024]
        PTa = A3[:, 33024:37120]
        qsl = A3[:, 37120:38144]
        ansst = A3[0:65, 38144:39168]
        mask2 = A3[:, 39168:40192].bitcast(F32)
        lstg = A3[0:65, 40192:42240].bitcast(F32)
        l8 = A3[0:8, 0:8192].bitcast(F32)
        blh = A3[0:64, 8192:12288]
        blst = A3[0:64, 12288:14336]
        anr = A3[0:64, 14336:18432]
        ansl = A3[0:64, 38144:42240]
        wosb = A3[0:64, 0:32768]
        dq1 = A4[:, 0:4096].bitcast(F32)
        dq2 = A4[:, 4096:8192].bitcast(F32)
        dq3 = A4[:, 8192:10240]
        t1b = A4[:, 0:4096].bitcast(F32)
        wost = A4[0:64, 0:2048].bitcast(F32)
        xfin = A4[:, 4096:6144].bitcast(F32)
        boutbc = A3[:, 32768:36864].bitcast(F32)
        oqsl = A4[:, 6144:12288]

        sem_names = [
            "d_x", "d_wq", "d_b", "d_wn", "d_xn16", "d_xnT", "d_xm", "d_xq",
            "d_xqT", "d_wdq", "d_WTu", "d_WTg", "d_WTd", "d_fup", "d_fga",
            "d_p2", "d_fg2", "d_p3", "d_p4", "d_foq", "d_oq", "d_wo", "d_xf",
            "d_out", "d_ans", "d_l", "d_l8", "d_ci", "d_co", "d_exi", "d_abl",
            "d_qT", "d_q", "d_am",
            "c_A", "c_B", "c_C", "c_D", "c_AG",
            "e_stp", "e_sq", "e_var", "e_sd", "e_xn", "e_cast", "e_qkv",
            "e_qev", "e_vev", "e_sc", "e_exp", "e_av", "e_avev", "e_deq",
            "e_scl", "e_xq", "e_fev", "e_fmx", "e_sig", "e_ew", "e_p3",
            "e_p4", "e_nrm", "e_bl", "e_dp", "e_op", "e_fin", "e_il",
            "e_msk", "e_vbc", "e_gmm", "e_sA", "e_sB", "e_sC", "e_sD",
            "e_upm", "e_woc", "e_bb", "e_p3g", "e_vch", "e_gch",
            "e_dqr", "e_dqg", "e_dqb",
        ]
        sems = {n: E(nc.semaphore(n)) for n in sem_names}
        cnt = {n: 0 for n in sem_names}
        prog = {k: [] for k in ("sync", "scalar", "vector", "tensor", "gpsimd")}

        def em(eng, fn):
            prog[eng].append(fn)

        def inc(name, k=1):
            cnt[name] += k
            return cnt[name]

        def wt(eng, name, val=None):
            v = cnt[name] if val is None else val
            if v <= 0:
                return
            em(eng, lambda e, s=sems[name], v=v: e.wait_ge(s, v))

        def dma(eng, out, in_, sem, transpose=False):
            s = sems[sem]
            if transpose:
                em(eng, lambda e, o=out, i=in_, s=s:
                   e.dma_start_transpose(o, i).then_inc(s, 16))
            else:
                em(eng, lambda e, o=out, i=in_, s=s:
                   e.dma_start(out=o, in_=i).then_inc(s, 16))
            return inc(sem, 16)

        def mm(out, lhsT, rhs, start, stop, tp=None, sem=None):
            def f(t, o=out, l=lhsT, r=rhs, st=start, sp=stop, tp=tp, sem=sem):
                i = t.matmul(out=o, lhsT=l, rhs=r, start=st, stop=sp,
                             tile_position=tp)
                if sem is not None:
                    i.then_inc(sems[sem], 1)
            em("tensor", f)
            if sem is not None:
                inc(sem)

        def ts(eng, out, in0, s1, op0, s2=None, op1=None, sem=None):
            def f(e, o=out, i=in0, s1=s1, s2=s2, op0=op0, op1=op1, sem=sem):
                if op1 is not None:
                    r = e.tensor_scalar(out=o, in0=i, scalar1=s1, scalar2=s2,
                                        op0=op0, op1=op1)
                else:
                    r = e.tensor_scalar(out=o, in0=i, scalar1=s1, scalar2=None,
                                        op0=op0)
                if sem is not None:
                    r.then_inc(sems[sem], 1)
            em(eng, f)
            if sem is not None:
                inc(sem)

        def tt(eng, out, in0, in1, op, sem=None):
            def f(e, o=out, a=in0, b=in1, op=op, sem=sem):
                r = e.tensor_tensor(out=o, in0=a, in1=b, op=op)
                if sem is not None:
                    r.then_inc(sems[sem], 1)
            em(eng, f)
            if sem is not None:
                inc(sem)

        def cp(eng, out, in_, sem=None):
            def f(e, o=out, i=in_, sem=sem):
                r = e.tensor_copy(out=o, in_=i)
                if sem is not None:
                    r.then_inc(sems[sem], 1)
            em(eng, f)
            if sem is not None:
                inc(sem)

        em("gpsimd", lambda g: g.memset(mask2[:], 0.0).then_inc(sems["e_stp"], 1))
        inc("e_stp")
        wt("gpsimd", "e_stp")
        em("gpsimd", lambda g: g.affine_select(
            out=mask2[:], in_=mask2[:], compare_op=ALU.is_ge, fill=NEG,
            base=-384, pattern=[[1, 512]], channel_multiplier=-1)
            .then_inc(sems["e_stp"], 1))
        inc("e_stp")
        em("gpsimd", lambda g: g.memset(onehot[:], 1.0).then_inc(sems["e_stp"], 1))
        inc("e_stp")
        wt("gpsimd", "e_stp")
        em("gpsimd", lambda g: g.affine_select(
            out=onehot[:], in_=onehot[:], compare_op=ALU.is_ge, fill=0.0,
            base=0, pattern=[[1, HPC * 64]], channel_multiplier=-64)
            .then_inc(sems["e_stp"], 1))
        inc("e_stp")
        wt("gpsimd", "e_stp")
        em("gpsimd", lambda g: g.affine_select(
            out=onehot[:], in_=onehot[:], compare_op=ALU.is_ge, fill=0.0,
            base=63, pattern=[[-1, HPC * 64]], channel_multiplier=64)
            .then_inc(sems["e_stp"], 1))
        inc("e_stp")
        em("vector", lambda v: v.memset(onesf, 1.0))
        em("vector", lambda v: v.memset(constv[:, 0:1], 1e-5))
        em("vector", lambda v: v.memset(
            VT.rearrange("p (q e) -> p q e", e=65)[:, :, 64:65], 1.0))
        em("vector", lambda v: v.engine_nop().then_inc(sems["e_stp"], 1))
        inc("e_stp")
        em("sync", lambda e: e.dma_start(
            out=bq_sb[:], in_=bqkv_h[0, :].rearrange("(m p) -> p m", p=128),
            allow_slow_non_contiguous=True).then_inc(sems["d_b"], 16))
        inc("d_b", 16)
        dma("sync", s1t[0:1, 272:784], bqkv_h[0:1, 1024:1536], "d_b")
        dma("sync", s1t[0:1, 784:1808], bout_h[0:1, :], "d_b")
        dma("sync", hselb[0:1, :], hsel_h[0:1, :], "d_b")
        wt("tensor", "d_b")
        for nn in range(4):
            mm(bank(6)[:, 128 * nn:128 * nn + 128], onesf,
               s1t[0:1, 272 + 128 * nn: 272 + 128 * nn + 128], True, True,
               sem="e_qkv")
        mm(bank(7)[:, 0:2], onesf, hselb[0:1, 0:2], True, True,
           sem="e_qkv")
        wt("vector", "e_qkv")
        em("vector", lambda v: v.tensor_copy(
            out=vbias[:, 0:520].rearrange("p (g e) -> p g e", e=65)[:, :, 0:64],
            in_=bank(6).rearrange("p (g e) -> p g e", e=64)))
        em("vector", lambda v: v.memset(
            vbias[:, 0:520].rearrange("p (g e) -> p g e", e=65)[:, :, 64:65], 0.0))
        cp("vector", hselb[:, :], bank(7)[:, 0:2], sem="e_vbc")

        def emit_ln_tile(i, src_h, is_mine):
            ri = i % 2
            bi = ri
            sb = 8 * (i % 2)
            ST = lambda k: stats[:, sb + k: sb + k + 1]
            dsem = "d_xm" if is_mine else "d_x"
            gsem = "e_xq" if is_mine else "e_xn"
            if i >= 2:
                wt("sync", gsem, cnt[gsem] - 1)
            dma("sync", xring[:, 1024 * ri: 1024 * ri + 1024],
                src_h[128 * i: 128 * i + 128, :], dsem)
            wt("scalar", dsem)
            if i >= 2:
                wt("scalar", "e_var", cnt["e_var"] - 1)
            em("scalar", lambda sc, ri=ri, ST=ST: sc.activation(
                out=xnbuf[:, 1024 * ri: 1024 * ri + 1024],
                in_=xring[:, 1024 * ri: 1024 * ri + 1024],
                func=AF.Square, accum_out=ST(0)).then_inc(sems["e_sq"], 1))
            inc("e_sq")
            wt("vector", dsem)
            em("vector", lambda v, ri=ri, ST=ST: v.reduce_sum(
                out=ST(1), in_=xring[:, 1024 * ri: 1024 * ri + 1024],
                axis=AX.X).then_inc(sems["e_vch"], 1))
            inc("e_vch")
            wt("vector", "e_vch")
            ts("vector", ST(2), ST(1), 1.0 / D, ALU.mult, sem="e_vch")
            wt("vector", "e_vch")
            tt("vector", ST(3), ST(2), ST(2), ALU.mult, sem="e_vch")
            wt("vector", "e_vch")
            wt("vector", "e_sq")
            em("vector", lambda v, ST=ST: v.scalar_tensor_tensor(
                out=ST(4), in0=ST(0), scalar=1.0 / D,
                in1=ST(3), op0=ALU.mult, op1=ALU.subtract)
                .then_inc(sems["e_var"], 1))
            inc("e_var")
            wt("scalar", "e_var")
            em("scalar", lambda sc, ST=ST: sc.activation(
                out=ST(5), in_=ST(4), func=AF.Sqrt,
                bias=constv[:, 0:1]).then_inc(sems["e_sd"], 1))
            inc("e_sd")
            wt("vector", "e_sd")
            em("vector", lambda v, ST=ST: v.reciprocal(out=ST(6), in_=ST(5))
               .then_inc(sems["e_vch"], 1))
            inc("e_vch")
            wt("vector", "e_vch")
            ts("vector", xnbuf[:, 1024 * bi: 1024 * bi + 1024],
               xring[:, 1024 * ri: 1024 * ri + 1024], ST(2),
               ALU.subtract, ST(6), ALU.mult, sem="e_vch")
            wt("vector", "e_vch")
            if not is_mine:
                cp("vector", xn16b[:, 1024 * bi: 1024 * bi + 1024],
                   xnbuf[:, 1024 * bi: 1024 * bi + 1024])
                em("vector", lambda v, bi=bi, ST=ST: v.reduce_max(
                    out=ST(7),
                    in_=xnbuf[:, 1024 * bi: 1024 * bi + 1024], axis=AX.X,
                    apply_absolute_value=True).then_inc(sems["e_vch"], 1))
                inc("e_vch")
                wt("vector", "e_vch")
                wt("vector", "e_xn")
                if i == 0:
                    cp("vector", mxt[:, 0:1], ST(7), sem="e_xn")
                else:
                    tt("vector", mxt[:, 0:1], mxt[:, 0:1], ST(7),
                       ALU.max, sem="e_xn")
                wt("sync", "e_xn")
                dma("sync", xn16_h[128 * i: 128 * i + 128, :],
                    xn16b[:, 1024 * bi: 1024 * bi + 1024], "d_xn16")
            else:
                wt("vector", "e_scl", 1)
                ts("vector", xnbuf[:, 1024 * bi: 1024 * bi + 1024],
                   xnbuf[:, 1024 * bi: 1024 * bi + 1024], scl[:, 1:2],
                   ALU.mult, MAGIC, ALU.add, sem="e_vch")
                wt("vector", "e_vch")
                ts("vector", xnbuf[:, 1024 * bi: 1024 * bi + 1024],
                   xnbuf[:, 1024 * bi: 1024 * bi + 1024], -MAGIC, ALU.add,
                   sem="e_vch")
                wt("vector", "e_vch")
                ts("vector", xq16b[:, 1024 * bi: 1024 * bi + 1024],
                   xnbuf[:, 1024 * bi: 1024 * bi + 1024], scl[:, 0:1],
                   ALU.mult, sem="e_xq")
                wt("sync", "e_xq")
                dma("sync", xq16_h[128 * i: 128 * i + 128, :],
                    xq16b[:, 1024 * bi: 1024 * bi + 1024], "d_xq")

        for i in range(NT):
            emit_ln_tile(i, x_h, False)

        for c in range(DC):
            if c >= 1:
                wt("sync", "e_cast", c)
            dma("sync", wq32[:, :], wqkv_h[128 * c: 128 * c + 128, :], "d_wq")
            wt("vector", "d_wq")
            cp("vector", wqb[:, 1536 * c: 1536 * c + 1536], wq32[:, :],
               sem="e_cast")

        qkv_group_done = [0, 0, 0, 0]
        for g in range(4):
            wt("scalar", "d_xn16", 16 * 8 * (g + 1))
            if g >= 1:
                wt("scalar", "e_qkv", qkv_group_done[g - 1])
            for c in range(DC):
                dma("scalar", xnT[:, c * 1024: c * 1024 + 1024],
                    xn16_h[1024 * g: 1024 * g + 1024, 128 * c: 128 * c + 128],
                    "d_xnT", transpose=True)
            wt("tensor", "d_xnT")
            wt("tensor", "e_cast", 8)
            for m in range(8):
                for nn in range(2):
                    bk = (m * 2 + nn) % 2
                    wt("tensor", "e_qev")
                    for c in range(DC):
                        mm(bank(bk),
                           wqb[:, 1536 * c + 128 * m: 1536 * c + 128 * m + 128],
                           xnT[:, c * 1024 + 512 * nn: c * 1024 + 512 * nn + 512],
                           c == 0, c == DC - 1,
                           sem="e_qkv" if c == DC - 1 else None)
                    wt("vector", "e_qkv")
                    if m < 4:
                        qslot = (cnt["d_qT"] // 16) % 2
                        wt("vector", "d_qT", cnt["d_qT"] - 16)
                        ts("vector", qstg[:, 512 * qslot: 512 * qslot + 512],
                           bank(bk), bq_sb[:, m:m + 1], ALU.add, sem="e_qev")
                        wt("sync", "e_qev")
                        dma("sync", qT_h[128 * m: 128 * m + 128,
                                         1024 * g + 512 * nn:
                                         1024 * g + 512 * nn + 512],
                            qstg[:, 512 * qslot: 512 * qslot + 512], "d_qT")
                    else:
                        mloc = m - 4
                        col = 4096 * mloc + 1024 * g + 512 * nn
                        ts("vector", KT[:, col: col + 512], bank(bk),
                           bq_sb[:, m:m + 1], ALU.add, sem="e_qev")
            for tl in range(8):
                tti = 8 * g + tl
                bk = 2 + tl % 2
                wt("tensor", "e_vev")
                for c in range(DC):
                    mm(bank(bk),
                       xnT[:, c * 1024 + 128 * tl: c * 1024 + 128 * tl + 128],
                       wqb[:, 1536 * c + 1024: 1536 * c + 1536],
                       c == 0, c == DC - 1,
                       sem="e_qkv" if c == DC - 1 else None)
                wt("vector", "e_qkv")
                if g == 0 and tl == 0:
                    wt("vector", "e_vbc", 1)
                vtile = VT[:, 520 * tti: 520 * tti + 520]
                tt("vector",
                   vtile.rearrange("p (gg e) -> p gg e", gg=HPC)[:, :, 0:64],
                   bank(bk).rearrange("p (gg e) -> p gg e", gg=HPC),
                   vbias[:, 0:520].rearrange("p (gg e) -> p gg e", e=65)[:, :, 0:64],
                   ALU.add, sem="e_vev")
            qkv_group_done[g] = cnt["e_qkv"]
        QKV_ALL = cnt["e_qkv"]
        VEV_ALL = cnt["e_vev"]
        QT_ALL = cnt["d_qT"]

        QEV_ALL = cnt["e_qev"]

        deq_base = cnt["e_deq"]
        wn_base = cnt["d_wn"]
        dq_base = cnt["d_wdq"]
        deq_specs = []
        for src_, dst_, n_chunks, width, wslots, col_halves in (
                (wu_h, w16_h["u"], FC, D, 2, 1),
                (wg_h, w16_h["g"], FC, D, 2, 1),
                (wd_h, w16_h["d"], 2 * DC, FF // 2, 1, 2)):
            for cchunk in range(n_chunks):
                deq_specs.append((src_, dst_, cchunk // col_halves,
                                  (cchunk % col_halves) * width, width,
                                  wslots))

        def deq_views(k):
            src_, dst_, row, c0, width, wslots = deq_specs[k]
            nb = width // 32
            rc = k % wslots
            return (src_, dst_, row, c0, width, wslots, nb,
                    dq1[:, width * rc: width * rc + width],
                    dq2[:, width * rc: width * rc + width],
                    sblk[:, 192 * rc: 192 * rc + nb],
                    sblk[:, 192 * rc + 96: 192 * rc + 96 + nb])

        def emit_deq_load(k):
            src_, dst_, row, c0, width, wslots, nb, wn, wk, sb0, sb1 = \
                deq_views(k)
            if k >= wslots:
                em("gpsimd", lambda g, v=dq_base + 16 * (k - wslots + 1):
                   g.wait_ge(sems["d_wdq"], v))
            em("gpsimd", lambda g, s=src_, row=row, c0=c0, width=width, wn=wn:
               g.dma_start(out=wn,
                           in_=s[128 * row:128 * row + 128, c0:c0 + width])
               .then_inc(sems["d_wn"], 16))

        def emit_deq_tt(k):
            src_, dst_, row, c0, width, wslots, nb, wn, wk, sb0, sb1 = \
                deq_views(k)
            rc = k % wslots
            wq_ = dq3[:, width * rc: width * rc + width]
            em("gpsimd", lambda g, v=k + 1: g.wait_ge(sems["e_dqr"], v))
            if k >= 1:
                em("gpsimd", lambda g, v=k: g.wait_ge(sems["e_dqb"], v))
            em("gpsimd", lambda g, wn=wn, wk=wk, sb1=sb1, nb=nb:
               g.tensor_tensor(
                   out=wk.rearrange("p (b e) -> p b e", e=32),
                   in0=wn.rearrange("p (b e) -> p b e", e=32),
                   in1=sb1[:, :, None].to_broadcast([128, nb, 32]),
                   op=ALU.mult).then_inc(sems["e_dqg"], 1))
            em("gpsimd", lambda g, v=k + 1: g.wait_ge(sems["e_dqb"], v))
            em("gpsimd", lambda g, wk=wk, wq_=wq_, sb0=sb0, nb=nb:
               g.tensor_tensor(
                   out=wq_.rearrange("p (b e) -> p b e", e=32),
                   in0=wk.rearrange("p (b e) -> p b e", e=32),
                   in1=sb0[:, :, None].to_broadcast([128, nb, 32]),
                   op=ALU.mult))
            em("gpsimd", lambda g, d=dst_, row=row, c0=c0, width=width,
               wq_=wq_:
               g.dma_start(out=d[128 * row:128 * row + 128, c0:c0 + width],
                           in_=wq_).then_inc(sems["d_wdq"], 16))

        def task_deq_a(k):
            src_, dst_, row, c0, width, wslots, nb, wn, wk, sb0, sb1 = \
                deq_views(k)
            em("vector", lambda v, val=wn_base + 16 * (k + 1):
               v.wait_ge(sems["d_wn"], val))
            em("vector", lambda v, wn=wn, sb0=sb0: v.tensor_reduce(
                out=sb0, in_=wn.rearrange("p (b e) -> p b e", e=32),
                axis=AX.X, op=ALU.max, apply_absolute_value=True)
               .then_inc(sems["e_vch"], 1))
            inc("e_vch")
            wt("vector", "e_vch")
            em("vector", lambda v, sb0=sb0: v.tensor_scalar(
                out=sb0, in0=sb0, scalar1=1.0 / QW, scalar2=1e-12,
                op0=ALU.mult, op1=ALU.max).then_inc(sems["e_vch"], 1))
            inc("e_vch")
            wt("vector", "e_vch")
            em("vector", lambda v, sb0=sb0, sb1=sb1: v.reciprocal(
                out=sb1, in_=sb0).then_inc(sems["e_dqr"], 1))

        def task_deq_b(k):
            src_, dst_, row, c0, width, wslots, nb, wn, wk, sb0, sb1 = \
                deq_views(k)
            em("vector", lambda v, val=k + 1:
               v.wait_ge(sems["e_dqg"], val))
            em("vector", lambda v, wk=wk: v.tensor_scalar(
                out=wk, in0=wk, scalar1=MAGIC, scalar2=-MAGIC,
                op0=ALU.add, op1=ALU.add).then_inc(sems["e_dqb"], 1))

        emit_deq_load(0)
        emit_deq_load(1)
        wt("gpsimd", "e_xn", NT)
        em("gpsimd", lambda g: g.dma_start(out=cc_i["A"].ap()[:, :],
                                           in_=mxt[:, 0:1])
           .then_inc(sems["d_ci"], 16))
        inc("d_ci", 16)
        wt("gpsimd", "d_ci")
        em("gpsimd", lambda g: g.collective_compute(
            "AllReduce", ALU.max, replica_groups=[list(range(8))],
            ins=[cc_i["A"].ap().opt()], outs=[cc_o["A"].ap().opt()])
            .then_inc(sems["c_A"], 1))
        inc("c_A")
        wt("gpsimd", "c_A")
        em("gpsimd", lambda g: g.dma_start(
            out=s1t[0:1, 0:128], in_=cc_o["A"].ap()[:, 0].unsqueeze(0))
            .then_inc(sems["d_co"], 16))
        inc("d_co", 16)
        for k in range(0, 2 * FC - 2):
            emit_deq_tt(k)
            emit_deq_load(k + 2)
        emit_deq_tt(2 * FC - 2)
        emit_deq_tt(2 * FC - 1)
        for k in range(2 * FC, 64):
            emit_deq_load(k)
            emit_deq_tt(k)
        WU16_ALL = dq_base + 16 * FC
        WG16_ALL = dq_base + 16 * 2 * FC
        WD16_ALL = dq_base + 16 * 64
        cnt["d_wdq"] = WD16_ALL
        cnt["d_wn"] = wn_base + 16 * 64
        cnt["e_dqr"] = cnt["e_dqg"] = cnt["e_dqb"] = 64

        def task_scale_A():
            wt("vector", "d_co", 16)
            em("vector", lambda v: v.reduce_max(out=s1t[0:1, 128:129],
                                                in_=s1t[0:1, 0:128],
                                                axis=AX.X)
               .then_inc(sems["e_vch"], 1))
            inc("e_vch")
            wt("vector", "e_vch")
            ts("vector", s1t[0:1, 129:130], s1t[0:1, 128:129], 1.0 / QA,
               ALU.mult, 1e-12, ALU.max, sem="e_vch")
            wt("vector", "e_vch")
            em("vector", lambda v: v.reciprocal(
                out=s1t[0:1, 130:131], in_=s1t[0:1, 129:130])
                .then_inc(sems["e_sA"], 1))
            inc("e_sA")
            wt("tensor", "e_sA", 1)
            mm(bank(6)[:, 0:2], onesf, s1t[0:1, 129:131], True, True,
               sem="e_qkv")
            wt("vector", "e_qkv")
            cp("vector", scl[:, 0:2], bank(6)[:, 0:2], sem="e_scl")

        bg_tasks = []
        for k in range(FC):
            bg_tasks.append(lambda k=k: task_deq_a(k))
            if k >= 1:
                bg_tasks.append(lambda k=k - 1: task_deq_b(k))
        bg_tasks.append(task_scale_A)
        bg_tasks.append(lambda: task_deq_b(FC - 1))
        for i in range(16):
            bg_tasks.append(lambda i=i: emit_ln_tile(i, xm_h, True))
        for k in range(FC, 2 * FC):
            bg_tasks.append(lambda k=k: task_deq_a(k))
            if k >= FC + 1:
                bg_tasks.append(lambda k=k - 1: task_deq_b(k))
        bg_tasks.append(lambda: task_deq_b(2 * FC - 1))
        for k in range(2 * FC, 64):
            bg_tasks.append(lambda k=k: task_deq_a(k))
            bg_tasks.append(lambda k=k: task_deq_b(k))

        def drain(n):
            for _ in range(min(n, len(bg_tasks))):
                bg_tasks.pop(0)()

        def emit_attention_pair(jj, dsched=None):
            for qc in range(QCH):
                qlslot = (cnt["d_q"] // 16) % 2
                if cnt["d_q"] >= 32:
                    wt("scalar", "e_sc", qsl_guard[(cnt["d_q"] // 16) - 2])
                dma("scalar", qsl[:, 512 * qlslot: 512 * qlslot + 512],
                    qT_h[128 * jj: 128 * jj + 128,
                         512 * qc: 512 * qc + 512], "d_q")
                wt("tensor", "d_q")
                for u in range(2 * qc + 2):
                    ub = u % 2
                    if u == 0:
                        wt("tensor", "e_avev")
                    scnt = [0, 0]
                    mcnt = [0, 0]
                    for dt in range(2):
                        t = 2 * u + dt
                        if cnt["e_exp"] > 0:
                            wt("tensor", "e_exp",
                               cnt["e_exp"] - (1 if dt == 0 else 0))
                        for p in range(2):
                            mm(bank(2 * dt + p),
                               KT[64 * p:64 * p + 64,
                                  4096 * jj + 128 * t:
                                  4096 * jj + 128 * t + 128],
                               qsl[64 * p:64 * p + 64,
                                   512 * qlslot: 512 * qlslot + 512],
                               True, True, tp=(64 * p, 0),
                               sem="e_sc" if p == 1 else None)
                        scnt[dt] = cnt["e_sc"]
                        r = t - 4 * qc
                        if r >= 0:
                            wt("vector", "e_sc")
                            for p in range(2):
                                tt("vector",
                                   bank(2 * dt + p)[:, 0:128 * (r + 1)],
                                   bank(2 * dt + p)[:, 0:128 * (r + 1)],
                                   mask2[:, 384 - 128 * r: 512],
                                   ALU.add, sem="e_msk" if p == 1 else None)
                        mcnt[dt] = cnt["e_msk"]
                    if cnt["e_av"] > 2:
                        wt("scalar", "e_av", cnt["e_av"] - 2)
                    pbuf = 33024 + ub * 2048
                    for dt in range(2):
                        wt("scalar", "e_sc", scnt[dt])
                        wt("scalar", "e_msk", mcnt[dt])
                        PTh = A3[:, pbuf + 1024 * dt: pbuf + 1024 * dt + 1024]
                        em("scalar", lambda sc, dt=dt, PTh=PTh:
                           sc.activation(
                               out=PTh, in_=bank(2 * dt, 2), func=AF.Exp,
                               scale=0.125).then_inc(sems["e_exp"], 1))
                        inc("e_exp")
                    wt("tensor", "e_exp")
                    for p in range(2):
                        gloc = 2 * jj + p
                        for dt in range(2):
                            t = 2 * u + dt
                            mm(PS[0:65, 512 * (4 + p): 512 * (4 + p) + 512],
                               VT[:, 520 * t + 65 * gloc:
                                  520 * t + 65 * gloc + 65],
                               A3[:, pbuf + 1024 * dt + 512 * p:
                                  pbuf + 1024 * dt + 512 * p + 512],
                               t == 0, t == 4 * qc + 3,
                               sem="e_av" if dt == 1 else None)
                for p in range(2):
                    gloc = 2 * jj + p
                    ks = (cnt["d_ans"] // 16) % 2
                    wt("vector", "e_av")
                    if cnt["d_ans"] >= 32:
                        wt("vector", "d_ans", cnt["d_ans"] - 16)
                    if cnt["d_l"] >= 32:
                        wt("vector", "d_l", cnt["d_l"] - 16)
                    cp("vector", ansst[0:65, 512 * ks: 512 * ks + 512],
                       PS[0:65, 512 * (4 + p): 512 * (4 + p) + 512])
                    cp("vector", lstg[0:65, 512 * ks: 512 * ks + 512],
                       PS[0:65, 512 * (4 + p): 512 * (4 + p) + 512],
                       sem="e_avev")
                    wt("scalar", "e_avev")
                    dma("scalar", ansr_h[64 * gloc: 64 * gloc + 64,
                                         512 * qc: 512 * qc + 512],
                        ansst[0:64, 512 * ks: 512 * ks + 512], "d_ans")
                    dma("scalar", lbuf_h[0:1, 4096 * gloc + 512 * qc:
                                         4096 * gloc + 512 * qc + 512],
                        lstg[64:65, 512 * ks: 512 * ks + 512], "d_l")
                qsl_guard.append(cnt["e_sc"])
                drain(dsched[qc] if dsched is not None else qc + 3)

        qsl_guard = []

        def emit_ffn_mm(which):
            out_hbm = fup_h if which == "u" else fga_h
            wtsem = "d_WTu" if which == "u" else "d_WTg"
            dsem = "d_fup" if which == "u" else "d_fga"
            mcol = 1 if which == "u" else 2
            wt("tensor", wtsem, cnt[wtsem])
            wt("tensor", "d_xqT", cnt["d_xqT"])
            wt("tensor", "e_scl", 1)
            for m in range(FC):
                for qc4 in range(4):
                    bk = 6 + (m * 4 + qc4) % 2
                    wt("tensor", "e_fev", cnt["e_fev"] - 1)
                    for c in range(DC):
                        mm(bank(bk),
                           WT[:, 3072 * c + 128 * m: 3072 * c + 128 * m + 128],
                           xqT[:, TOK * c + 512 * qc4:
                               TOK * c + 512 * qc4 + 512],
                           c == 0, c == DC - 1,
                           sem="e_upm" if c == DC - 1 else None)
                    wt("vector", "e_upm")
                    wt("vector", dsem, cnt[dsem] - 16 * 3)
                    if qc4 == 0:
                        wt("vector", "e_fmx", cnt["e_fmx"] - 1)
                    cp("vector", fst[:, 512 * qc4: 512 * qc4 + 512],
                       bank(bk), sem="e_fev")
                    wt("sync", "e_fev")
                    dma("sync", out_hbm[128 * m:128 * m + 128,
                                        512 * qc4: 512 * qc4 + 512],
                        fst[:, 512 * qc4: 512 * qc4 + 512], dsem)
                wt("vector", "e_fev")
                em("vector", lambda v: v.tensor_reduce(
                    out=mxt[:, 4:8],
                    in_=fst[:, 0:2048].rearrange("p (b e) -> p b e", b=4),
                    axis=AX.X, op=ALU.max, apply_absolute_value=True)
                   .then_inc(sems["e_vch"], 1))
                inc("e_vch")
                wt("vector", "e_vch")
                em("vector", lambda v: v.tensor_reduce(
                    out=mxt[:, 3:4], in_=mxt[:, 4:8], axis=AX.X,
                    op=ALU.max).then_inc(sems["e_vch"], 1))
                inc("e_vch")
                wt("vector", "e_vch")
                wt("vector", "e_fmx")
                if m == 0:
                    cp("vector", mxt[:, mcol:mcol + 1], mxt[:, 3:4],
                       sem="e_fmx")
                else:
                    tt("vector", mxt[:, mcol:mcol + 1], mxt[:, mcol:mcol + 1],
                       mxt[:, 3:4], ALU.max, sem="e_fmx")

        wt("scalar", "d_qT", QT_ALL)
        wt("tensor", "e_qev", QEV_ALL)
        wt("tensor", "e_vev", VEV_ALL)
        wt("tensor", "e_qkv", QKV_ALL)
        wt("tensor", "e_vbc", 1)
        wt("vector", "e_stp", cnt["e_stp"])
        emit_attention_pair(0)
        emit_attention_pair(1)
        wt("sync", "e_qkv", QKV_ALL)
        wt("sync", "e_cast", 8)
        wt("sync", "d_wdq", WU16_ALL)
        for c in range(DC):
            dma("sync", WT[:, 3072 * c: 3072 * c + 3072],
                w16_h["u"][:, 128 * c: 128 * c + 128], "d_WTu",
                transpose=True)
        wt("sync", "d_xq", 16 * 16)
        wt("sync", "e_xq", cnt["e_xq"])
        for c in range(DC):
            dma("sync", xqT[:, TOK * c: TOK * c + TOK],
                xq16_h[:, 128 * c: 128 * c + 128], "d_xqT", transpose=True)
        emit_ffn_mm("u")
        UPM_ALL = cnt["e_upm"]
        emit_attention_pair(2)
        wt("sync", "d_wdq", WG16_ALL)
        wt("sync", "e_upm", UPM_ALL)
        for c in range(DC):
            dma("sync", WT[:, 3072 * c: 3072 * c + 3072],
                w16_h["g"][:, 128 * c: 128 * c + 128], "d_WTg",
                transpose=True)
        emit_ffn_mm("g")
        em("tensor", lambda t: t.nop().then_inc(sems["e_gmm"], 1))
        inc("e_gmm")
        FEV_ALL = cnt["e_fev"]

        def cc_round(key, incol, n):
            em("gpsimd", lambda g, key=key, incol=incol, n=n: g.dma_start(
                out=cc_i[key].ap()[:, :], in_=mxt[:, incol:incol + n])
                .then_inc(sems["d_ci"], 16))
            inc("d_ci", 16)
            wt("gpsimd", "d_ci")
            em("gpsimd", lambda g, key=key: g.collective_compute(
                "AllReduce", ALU.max, replica_groups=[list(range(8))],
                ins=[cc_i[key].ap().opt()], outs=[cc_o[key].ap().opt()])
                .then_inc(sems["c_" + key], 1))
            inc("c_" + key)
            wt("gpsimd", "c_" + key)
            em("gpsimd", lambda g, key=key, n=n: g.dma_start(
                out=s1t[0:1, 0:128 * n].rearrange("o (k p) -> o k p", k=n),
                in_=cc_o[key].ap()[:, :].rearrange("p k -> k p").unsqueeze(0),
                allow_slow_non_contiguous=True).then_inc(sems["d_co"], 16))
            inc("d_co", 16)

        def scale_calc(n_src, scol, esem):
            wt("vector", "d_co", cnt["d_co"])
            for k in range(n_src):
                em("vector", lambda v, k=k, n=n_src: v.reduce_max(
                    out=s1t[0:1, 128 * n + k: 128 * n + k + 1],
                    in_=s1t[0:1, 128 * k: 128 * k + 128], axis=AX.X)
                   .then_inc(sems["e_vch"], 1))
                inc("e_vch")
                wt("vector", "e_vch")
                ts("vector", s1t[0:1, 260 + 2 * k: 261 + 2 * k],
                   s1t[0:1, 128 * n_src + k: 128 * n_src + k + 1],
                   1.0 / QA, ALU.mult, 1e-12, ALU.max, sem="e_vch")
                wt("vector", "e_vch")
                em("vector", lambda v, k=k: v.reciprocal(
                    out=s1t[0:1, 261 + 2 * k: 262 + 2 * k],
                    in_=s1t[0:1, 260 + 2 * k: 261 + 2 * k]))
            em("vector", lambda v: v.engine_nop().then_inc(sems[esem], 1))
            inc(esem)
            wt("tensor", esem, cnt[esem])
            mm(bank(6)[:, 0:2 * n_src], onesf,
               s1t[0:1, 260: 260 + 2 * n_src], True, True, sem="e_qkv")
            wt("vector", "e_qkv")
            cp("vector", scl[:, scol: scol + 2 * n_src],
               bank(6)[:, 0:2 * n_src], sem="e_scl")


        def task_pass2(m):
            rm = m % 2
            eng = "vector"
            csem = "e_vch"
            einv = ein[:, 2048 * rm: 2048 * rm + 2048]
            ein2v = ein2[:, 2048 * rm: 2048 * rm + 2048]
            wt("sync", "c_B", 1)
            if m <= 1:
                wt("sync", "e_gmm", 1)
                wt("sync", "e_fev", FEV_ALL)
            if cnt["d_fg2"] >= 32:
                wt("sync", "d_fg2", cnt["d_fg2"] - 16)
            dma("sync", einv, fga_h[128 * m: 128 * m + 128, :], "d_p2")
            wt(eng, "d_p2")
            wt(eng, "e_scl", 2)
            em(eng, lambda e, einv=einv, s=sems[csem]: e.tensor_scalar(
                out=einv, in0=einv, scalar1=scl[:, 5:6], scalar2=MAGIC,
                op0=ALU.mult, op1=ALU.add).then_inc(s, 1))
            inc(csem)
            wt(eng, csem)
            em(eng, lambda e, einv=einv: e.tensor_scalar(
                out=einv, in0=einv, scalar1=-MAGIC, scalar2=scl[:, 4:5],
                op0=ALU.add, op1=ALU.mult).then_inc(sems["e_ew"], 1))
            inc("e_ew")
            wt("scalar", "e_ew")
            em("scalar", lambda sc, einv=einv, ein2v=ein2v: sc.activation(
                out=ein2v, in_=einv, func=AF.Sigmoid)
                .then_inc(sems["e_sig"], 1))
            inc("e_sig")
            wt(eng, "e_sig")
            em(eng, lambda e, ein2v=ein2v, s=sems[csem]: e.tensor_scalar(
                out=ein2v, in0=ein2v, scalar1=U16, scalar2=MAGIC,
                op0=ALU.mult, op1=ALU.add).then_inc(s, 1))
            inc(csem)
            wt(eng, csem)
            em(eng, lambda e, ein2v=ein2v, s=sems[csem]: e.tensor_scalar(
                out=ein2v, in0=ein2v, scalar1=-MAGIC, scalar2=65535.0,
                op0=ALU.add, op1=ALU.min).then_inc(s, 1))
            inc(csem)
            wt(eng, csem)
            em(eng, lambda e, ein2v=ein2v, s=sems[csem]: e.tensor_scalar(
                out=ein2v, in0=ein2v, scalar1=0.0, scalar2=1.0 / U16,
                op0=ALU.max, op1=ALU.mult).then_inc(s, 1))
            inc(csem)
            wt(eng, csem)
            em(eng, lambda e, einv=einv, ein2v=ein2v: e.tensor_tensor(
                out=einv, in0=einv, in1=ein2v, op=ALU.mult)
                .then_inc(sems["e_ew"], 1))
            inc("e_ew")
            wt("vector", "e_ew")
            em("vector", lambda v, einv=einv: v.tensor_reduce(
                out=mxt[:, 8:9], in_=einv.rearrange("p (b e) -> p b e", b=1),
                axis=AX.X, op=ALU.max, apply_absolute_value=True)
               .then_inc(sems["e_vch"], 1))
            inc("e_vch")
            wt("vector", "e_vch")
            if m == 0:
                cp("vector", mxt[:, 3:4], mxt[:, 8:9], sem="e_ew")
            else:
                wt("vector", "e_ew")
                tt("vector", mxt[:, 3:4], mxt[:, 3:4], mxt[:, 8:9], ALU.max,
                   sem="e_ew")
            wt("sync", "e_ew")
            dma("sync", fg2_h[128 * m: 128 * m + 128, :], einv, "d_fg2")


        wt("gpsimd", "e_fmx", 2 * FC)
        cc_round("B", 1, 2)
        bg_tasks.append(lambda: scale_calc(2, 2, "e_sB"))
        for m in range(FC):
            bg_tasks.append(lambda m=m: task_pass2(m))
        emit_attention_pair(3, dsched=(0, 0, 0, 0, 7, 6, 6, 6))
        drain(len(bg_tasks))
        SC_ALL = cnt["e_sc"]
        ANS_ALL = cnt["d_ans"]
        L_ALL = cnt["d_l"]
        AVEV_ALL = cnt["e_avev"]

        wt("scalar", "d_l", L_ALL)
        wt("scalar", "e_sc", SC_ALL)
        dma("scalar", l8[0:HPC, :],
            lbuf_h[0, :].rearrange("(g n) -> g n", g=HPC), "d_l8")
        wt("vector", "d_l8", 16)
        em("vector", lambda v: v.reciprocal(out=l8[0:HPC, :],
                                            in_=l8[0:HPC, :])
           .then_inc(sems["e_il"], 1))
        inc("e_il")
        for gl in range(HPC):
            wt("scalar", "d_exi", cnt["d_exi"])
            wt("scalar", "d_ans", ANS_ALL)
            if gl == 0:
                wt("scalar", "e_avev", AVEV_ALL)
            dma("scalar", anr[0:64, :], ansr_h[64 * gl: 64 * gl + 64, :],
                "d_abl")
            wt("tensor", "e_il", 1)
            for qq in range(4):
                bkq = qq % 2
                wt("tensor", "e_nrm", cnt["e_nrm"] - 1)
                for jh in range(2):
                    mm(PS[0:64,
                          512 * (2 * bkq + jh): 512 * (2 * bkq + jh) + 512],
                       onehot[0:HPC, 64 * gl: 64 * gl + 64],
                       l8[0:HPC, 1024 * qq + 512 * jh:
                          1024 * qq + 512 * jh + 512],
                       True, True, sem="e_bb" if jh == 1 else None)
                wt("vector", "e_bb")
                wt("vector", "d_abl")
                tt("vector", anr[0:64, 1024 * qq: 1024 * qq + 1024],
                   anr[0:64, 1024 * qq: 1024 * qq + 1024],
                   PS[0:64, 1024 * bkq: 1024 * bkq + 1024], ALU.mult,
                   sem="e_nrm")
            wt("scalar", "e_nrm")
            dma("scalar", exi_h.ap()[64 * gl: 64 * gl + 64, :],
                anr[0:64, :], "d_exi")

        wt("gpsimd", "e_ew", 3 * FC)
        cc_round("C", 3, 1)
        scale_calc(1, 6, "e_sC")

        for m in range(FC):
            rm = m % 2
            eng = "vector"
            einv = ein[:, 2048 * rm: 2048 * rm + 2048]
            ein2v = ein2[:, 2048 * rm: 2048 * rm + 2048]
            wt("sync", "c_C", 1)
            wt("sync", "e_p3", cnt["e_p3"] - 1)
            dma("sync", einv, fg2_h[128 * m: 128 * m + 128, :], "d_p3")
            wt("scalar", "c_C", 1)
            wt("scalar", "e_p3", cnt["e_p3"] - 1)
            dma("scalar", ein2v, fup_h[128 * m: 128 * m + 128, :], "d_fup")
            wt(eng, "d_p3")
            wt(eng, "d_fup", cnt["d_fup"])
            wt(eng, "e_scl", 3)
            csem = "e_gch" if eng == "gpsimd" else "e_vch"
            for s1a, s2a, o0, o1, tgt in (
                    (scl[:, 7:8], MAGIC, ALU.mult, ALU.add, einv),
                    (-MAGIC, scl[:, 6:7], ALU.add, ALU.mult, einv),
                    (scl[:, 3:4], MAGIC, ALU.mult, ALU.add, ein2v),
                    (-MAGIC, scl[:, 2:3], ALU.add, ALU.mult, ein2v)):
                em(eng, lambda e, t=tgt, s1a=s1a, s2a=s2a, o0=o0, o1=o1,
                   s=sems[csem]:
                   e.tensor_scalar(out=t, in0=t, scalar1=s1a, scalar2=s2a,
                                   op0=o0, op1=o1).then_inc(s, 1))
                inc(csem)
                wt(eng, csem)
            if eng == "gpsimd":
                em(eng, lambda e, einv=einv, ein2v=ein2v: e.tensor_tensor(
                    out=einv, in0=einv, in1=ein2v, op=ALU.mult)
                    .then_inc(sems["e_p3g"], 1))
                inc("e_p3g")
                wt("vector", "e_p3g")
            else:
                em(eng, lambda e, einv=einv, ein2v=ein2v: e.tensor_tensor(
                    out=einv, in0=einv, in1=ein2v, op=ALU.mult)
                    .then_inc(sems["e_vch"], 1))
                inc("e_vch")
                wt("vector", "e_vch")
            em("vector", lambda v, einv=einv: v.tensor_reduce(
                out=mxt[:, 8:9],
                in_=einv.rearrange("p (b e) -> p b e", b=1),
                axis=AX.X, op=ALU.max, apply_absolute_value=True)
               .then_inc(sems["e_vch"], 1))
            inc("e_vch")
            wt("vector", "e_vch")
            if m == 0:
                cp("vector", mxt[:, 4:5], mxt[:, 8:9], sem="e_p3")
            else:
                tt("vector", mxt[:, 4:5], mxt[:, 4:5], mxt[:, 8:9], ALU.max,
                   sem="e_p3")

        wt("gpsimd", "e_p3", FC)
        cc_round("D", 4, 1)
        scale_calc(1, 8, "e_sD")

        for m in range(FC):
            rm = m % 2
            eng = "vector"
            einv = ein[:, 2048 * rm: 2048 * rm + 2048]
            ein2v = ein2[:, 2048 * rm: 2048 * rm + 2048]
            eo16v = eo16[:, 2048 * rm: 2048 * rm + 2048]
            wt("sync", "c_D", 1)
            wt("sync", "e_p4", cnt["e_p4"] - 1)
            dma("sync", einv, fg2_h[128 * m: 128 * m + 128, :], "d_p4")
            wt("scalar", "c_D", 1)
            wt("scalar", "e_p4", cnt["e_p4"] - 1)
            dma("scalar", ein2v, fup_h[128 * m: 128 * m + 128, :], "d_fga")
            wt(eng, "d_p4")
            wt(eng, "d_fga", cnt["d_fga"])
            wt(eng, "e_scl", 4)
            if m <= 1:
                wt(eng, "e_fev", FEV_ALL)
            csem = "e_gch" if eng == "gpsimd" else "e_vch"
            for s1a, s2a, o0, o1, tgt in (
                    (scl[:, 7:8], MAGIC, ALU.mult, ALU.add, einv),
                    (-MAGIC, scl[:, 6:7], ALU.add, ALU.mult, einv),
                    (scl[:, 3:4], MAGIC, ALU.mult, ALU.add, ein2v),
                    (-MAGIC, scl[:, 2:3], ALU.add, ALU.mult, ein2v)):
                em(eng, lambda e, t=tgt, s1a=s1a, s2a=s2a, o0=o0, o1=o1,
                   s=sems[csem]:
                   e.tensor_scalar(out=t, in0=t, scalar1=s1a, scalar2=s2a,
                                   op0=o0, op1=o1).then_inc(s, 1))
                inc(csem)
                wt(eng, csem)
            em(eng, lambda e, einv=einv, ein2v=ein2v, s=sems[csem]:
               e.tensor_tensor(
                   out=einv, in0=einv, in1=ein2v, op=ALU.mult).then_inc(s, 1))
            inc(csem)
            wt(eng, csem)
            em(eng, lambda e, einv=einv, s=sems[csem]: e.tensor_scalar(
                out=einv, in0=einv, scalar1=scl[:, 9:10], scalar2=MAGIC,
                op0=ALU.mult, op1=ALU.add).then_inc(s, 1))
            inc(csem)
            wt(eng, csem)
            wt(eng, "d_foq", cnt["d_foq"] - 16)
            em(eng, lambda e, einv=einv, eo16v=eo16v: e.tensor_scalar(
                out=eo16v, in0=einv, scalar1=-MAGIC, scalar2=scl[:, 8:9],
                op0=ALU.add, op1=ALU.mult).then_inc(sems["e_p4"], 1))
            inc("e_p4")
            wt("sync", "e_p4")
            dma("sync", foq_h[128 * m: 128 * m + 128, :], eo16v, "d_foq")
        FOQ_ALL = cnt["d_foq"]
        P4_ALL = cnt["e_p4"]

        wt("gpsimd", "d_exi", cnt["d_exi"])
        em("gpsimd", lambda g: g.collective_compute(
            "AllGather", ALU.bypass,
            replica_groups=[[0, 1], [2, 3], [4, 5], [6, 7]],
            ins=[exi_h.ap().opt()], outs=[exo_h.ap().opt()])
            .then_inc(sems["c_AG"], 1))
        inc("c_AG")
        wt("scalar", "c_AG", 1)
        for gg in range(NH):
            wt("scalar", "d_am", cnt["d_am"])
            dma("scalar", blh[0:64, 0:2048],
                exo_h.ap()[64 * gg: 64 * gg + 64, 0:TOK], "d_abl")
            dma("scalar", blh[0:64, 2048:4096],
                exo_h.ap()[64 * gg: 64 * gg + 64, TOK: 2 * TOK], "d_abl")
            wt("vector", "d_abl")
            wt("vector", "d_am", cnt["d_am"])
            ts("vector", blst[0:64, :], blh[0:64, 0:2048],
               hselb[0:64, 0:1], ALU.mult, sem="e_vch")
            wt("vector", "e_vch")
            em("vector", lambda v: v.scalar_tensor_tensor(
                out=blst[0:64, :], in0=blh[0:64, 2048:4096],
                scalar=hselb[0:64, 1:2], in1=blst[0:64, :],
                op0=ALU.mult, op1=ALU.add).then_inc(sems["e_bl"], 1))
            inc("e_bl")
            wt("scalar", "e_bl")
            dma("scalar", ansm_h[64 * gg: 64 * gg + 64, :], blst[0:64, :],
                "d_am")
        BL_ALL = cnt["e_bl"]
        AM_ALL = cnt["d_am"]

        wt("sync", "d_wdq", WD16_ALL)
        wt("sync", "e_p4", P4_ALL)
        wt("sync", "e_bl", BL_ALL)
        for k in range(2 * NH):
            if k >= 1:
                wt("sync", "e_woc", cnt["e_woc"])
            dma("sync", wost[0:64, 0:512],
                wout_h[0:64, 512 * k: 512 * k + 512], "d_wo")
            wt("vector", "d_wo")
            cp("vector", wosb[0:64, 512 * k: 512 * k + 512],
               wost[0:64, 0:512], sem="e_woc")
        WOC_ALL = cnt["e_woc"]

        wt("scalar", "d_wdq", WD16_ALL)
        wt("scalar", "e_gmm", 1)
        for c in range(FC):
            dma("scalar", WT[:, 1024 * c: 1024 * c + 1024],
                w16_h["d"][:, 128 * c: 128 * c + 128], "d_WTd", transpose=True)

        wt("tensor", "e_nrm", cnt["e_nrm"])
        for nn in range(2):
            mm(bank(nn), onesf,
               s1t[0:1, 784 + 512 * nn: 784 + 512 * nn + 512], True, True,
               sem="e_qkv")
        wt("vector", "e_qkv")
        wt("vector", "d_wdq", WD16_ALL)
        cp("vector", boutbc[:, 0:512], bank(0))
        cp("vector", boutbc[:, 512:1024], bank(1), sem="e_bb")
        BB_ALL = cnt["e_bb"]

        for tti in range(16):
            rt = tti % 2
            wt("sync", "e_dp", cnt["e_dp"] - 2)
            wt("sync", "d_foq", FOQ_ALL)
            dma("sync",
                oqsl[:, 3072 * rt: 3072 * rt + 3072]
                .rearrange("p (c n) -> p c n", c=FC),
                foq_h[:, 128 * tti: 128 * tti + 128]
                .rearrange("(c p) n -> p c n", p=128), "d_oq")
            wt("sync", "e_fin", cnt["e_fin"])
            dma("sync", xfin[:, :], xm_h[128 * tti: 128 * tti + 128, :], "d_xf")
            wt("scalar", "d_am", AM_ALL)
            wt("scalar", "e_op", cnt["e_op"] - 2)
            dma("scalar",
                ansl[0:64, 2048 * rt: 2048 * rt + 2048]
                .rearrange("p (g n) -> p g n", g=NH),
                ansm_h[:, 128 * tti: 128 * tti + 128]
                .rearrange("(g p) n -> p g n", p=64), "d_q")
            wt("tensor", "d_oq")
            wt("tensor", "d_WTd", 16 * FC)
            wt("tensor", "e_bb", BB_ALL)
            wt("tensor", "e_fin", cnt["e_fin"] - 1)
            for nn in range(2):
                for c in range(FC):
                    mm(bank(2 * rt + nn),
                       oqsl[:, 3072 * rt + 128 * c: 3072 * rt + 128 * c + 128],
                       WT[:, 1024 * c + 512 * nn: 1024 * c + 512 * nn + 512],
                       c == 0, c == FC - 1,
                       sem="e_dp" if c == FC - 1 else None)
            wt("tensor", "d_q")
            wt("tensor", "e_woc", WOC_ALL)
            for nn in range(2):
                for gg in range(NH):
                    mm(bank(4 + 2 * rt + nn),
                       ansl[0:64, 2048 * rt + 128 * gg:
                            2048 * rt + 128 * gg + 128],
                       wosb[0:64, 1024 * gg + 512 * nn:
                            1024 * gg + 512 * nn + 512],
                       gg == 0, gg == NH - 1,
                       sem="e_op" if gg == NH - 1 else None)
            wt("vector", "d_xf")
            wt("vector", "e_dp")
            wt("vector", "e_op")
            if tti == 0:
                wt("vector", "e_woc", WOC_ALL)
            for nn in range(2):
                t1v = t1b[:, 1024 * rt + 512 * nn: 1024 * rt + 512 * nn + 512]
                tt("vector", t1v, bank(4 + 2 * rt + nn),
                   xfin[:, 512 * nn: 512 * nn + 512], ALU.add, sem="e_vch")
                wt("vector", "e_vch")
                tt("vector", t1v, t1v, bank(2 * rt + nn), ALU.add, sem="e_vch")
                wt("vector", "e_vch")
                tt("vector", t1v, t1v, boutbc[:, 512 * nn: 512 * nn + 512],
                   ALU.add, sem="e_fin" if nn == 1 else None)
            wt("sync", "e_fin")
            dma("sync", out_h[128 * tti: 128 * tti + 128, :],
                t1b[:, 1024 * rt: 1024 * rt + 1024], "d_out")

        wt("sync", "d_out", 16 * 16)
        if dbg:
            for nm, srct in (("dxq", xq16_h), ("dwu", w16_h["u"]),
                             ("dfga", fga_h), ("dfg2", fg2_h),
                             ("dfoq", foq_h), ("dfup", fup_h)):
                dma("sync", dbg_h[nm][:, :], srct[:, :], "d_b")
            wt("sync", "d_b", cnt["d_b"])

        with nc.Block() as block:

            @block.sync
            def _(eng):
                for f in prog["sync"]:
                    f(eng)

            @block.scalar
            def _(eng):
                for f in prog["scalar"]:
                    f(eng)

            @block.vector
            def _(eng):
                for f in prog["vector"]:
                    f(eng)

            @block.tensor
            def _(eng):
                for f in prog["tensor"]:
                    f(eng)

            @block.gpsimd
            def _(eng):
                for f in prog["gpsimd"]:
                    f(eng)

    return nc


_NC_CACHE = None


def _get_nc():
    global _NC_CACHE
    if _NC_CACHE is None:
        _NC_CACHE = build_nc()
    return _NC_CACHE


def kernel(L0_attn_x, L0_attn_w_qkv, L0_attn_b_qkv, L0_attn_w_out, L0_attn_b_out,
           L0_ffn_w_gate, L0_ffn_w_up, L0_ffn_w_down):
    x = np.ascontiguousarray(np.asarray(L0_attn_x, dtype=np.float32))
    wqkv = np.asarray(L0_attn_w_qkv, dtype=np.float32)
    bqkv = np.asarray(L0_attn_b_qkv, dtype=np.float32)
    wout = np.asarray(L0_attn_w_out, dtype=np.float32)
    bout = np.asarray(L0_attn_b_out, dtype=np.float32)
    wg = np.ascontiguousarray(np.asarray(L0_ffn_w_gate, dtype=np.float32))
    wu = np.ascontiguousarray(np.asarray(L0_ffn_w_up, dtype=np.float32))
    wd = np.ascontiguousarray(np.asarray(L0_ffn_w_down, dtype=np.float32))

    wo_arr = np.ascontiguousarray(wout.transpose(1, 0, 2).reshape(64, NH * D))
    bout_r = np.ascontiguousarray(bout.reshape(1, D))

    nc = _get_nc()
    in_maps = []
    for c in range(8):
        b, h = c // 2, c % 2
        heads = slice(8 * h, 8 * h + 8)
        wq_c = np.ascontiguousarray(wqkv[:, :, heads, :].reshape(D, 1536))
        bq_c = np.ascontiguousarray(bqkv[:, heads, :].reshape(1, 1536))
        xb = np.ascontiguousarray(x[b])
        xm = np.ascontiguousarray(x[b, 2048 * h: 2048 * h + 2048])
        hsel = np.array([[1.0 - h, float(h)]], dtype=np.float32)
        in_maps.append({
            "x": xb, "xmine": xm, "wqkv": wq_c, "bqkv": bq_c,
            "wout": wo_arr, "bout": bout_r, "hsel": hsel,
            "wg": wg, "wu": wu, "wd": wd,
        })
    res = run_bass_kernel_spmd(nc, in_maps, core_ids=list(range(8)))
    out = np.empty((B, S, D), dtype=np.float32)
    for c in range(8):
        b, h = c // 2, c % 2
        out[b, 2048 * h: 2048 * h + 2048] = res.results[c]["out"]
    return out
